# version 21
# baseline (speedup 1.0000x reference)
import os
import numpy as np
import ml_dtypes
import concourse.bass as bass
import concourse.mybir as mybir
from concourse.bass_utils import run_bass_kernel_spmd

F32 = mybir.dt.float32
F16 = mybir.dt.float16
AF = mybir.ActivationFunctionType
OP = mybir.AluOpType

T = 2048
DM = 1024
DI = 2048
DS = 16
DR = 64
TC = 256
NCH = T // TC
NKT = DM // 128
NDT = DI // 128
EPS = 1e-5

COL_D, COL_CB, COL_BDT, COL_CW = 0, 16, 32, 48
COL_SC, COL_BI = 112, 120


class DmaProxy:
    def __init__(self, h, dsem):
        self.h = h
        self.dsem = dsem
        self.n = 0

    def dma_start(self, **kw):
        ins = self.h.dma_start(**kw)
        ins.then_inc(self.dsem, 16)
        self.n += 1
        return ins


def build_nc():
    nc = bass.Bass("TRN2", target_bir_lowering=False,
                   dynamic_dma_scratch_size=1024)

    xT = nc.dram_tensor("xT", [DM, T], F32, kind="ExternalInput").ap()
    w_inT = nc.dram_tensor("w_inT", [DM, 2 * DI], F32, kind="ExternalInput").ap()
    w_outT = nc.dram_tensor("w_outT", [DI, DM], F32, kind="ExternalInput").ap()
    w_ffnT = nc.dram_tensor("w_ffnT", [DM, DM], F32, kind="ExternalInput").ap()
    w_xT = nc.dram_tensor("w_xT", [DI, 96], F32, kind="ExternalInput").ap()
    w_dtT = nc.dram_tensor("w_dtT", [DR, DI], F32, kind="ExternalInput").ap()
    alog_c = nc.dram_tensor("alog_c", [128, NDT * DS], F32, kind="ExternalInput").ap()
    rows_c = nc.dram_tensor("rows_c", [128, 128], F32, kind="ExternalInput").ap()
    out_fm = nc.dram_tensor("out_fm", [DM, T], F32, kind="ExternalOutput").ap()

    MK_DEBUG = int(os.environ.get("MK_DEBUG", "0"))
    dbg = {}
    if MK_DEBUG:
        for nm, shape, dt in [("dbg_x1", [DM, TC], F32),
                              ("dbg_xc", [DI, TC], F16),
                              ("dbg_sz", [DI, TC], F16),
                              ("dbg_yg", [DI, TC], F16),
                              ("dbg_xn", [DM, TC], F16),
                              ("dbg_xi", [DI, TC + 3], F16)]:
            dbg[nm] = nc.dram_tensor(nm, shape, dt, kind="ExternalOutput").ap()
    bcflat = nc.dram_tensor("bcflat", [32, TC], F16, kind="Internal").ap()
    w_in16 = nc.dram_tensor("w_in16", [4 * NKT, 128, 1024], F16, kind="Internal").ap()
    w_out16 = nc.dram_tensor("w_out16", [DI, DM], F16, kind="Internal").ap()

    from contextlib import ExitStack
    es = ExitStack()
    S = lambda name, shape, dt: es.enter_context(nc.sbuf_tensor(name, shape, dt))

    winb = S("winb", [128, 2, 1024], F16)
    wob = S("wob", [128, 2, DM], F16)
    sgnT = S("sgnT", [128, NKT, DM], F16)
    wxT = S("wxT", [128, NDT, 96], F16)
    wdtT = S("wdtT", [DR, DI], F16)
    acol = S("acol", [128, NDT * DS], F32)
    rows = S("rows", [128, 128], F32)
    ones16 = S("ones16", [128, 1], F16)
    ones32r = S("ones32r", [1, 128], F32)

    x_sb = S("x_sb", [128, 2, NKT, TC], F32)
    stage_w = x_sb[:, 0].rearrange("p k t -> p (k t)")
    xbf = S("xbf", [128, NKT, TC], F16)
    xq = S("xq", [128, NKT, TC], F16)
    xn = S("xn", [128, 2, NKT, TC], F16)
    x_inner = S("x_inner", [128, NDT, TC + 3], F16)
    x_conv = S("x_conv", [128, 2, NDT, TC], F16)
    sz = S("sz", [128, 2, NDT, TC], F16)
    dtr = S("dtr", [DR, 2, TC], F16)
    bc_sb = S("bc_sb", [32, TC], F16)
    Brep = S("Brep", [128, 2, DS, TC], F16)
    Crep = S("Crep", [128, 2, DS, TC], F16)
    ygate = S("ygate", [128, NDT, TC], F16)
    dA = S("dA", [128, 2, DS, TC], F16)
    w_t = S("w_t", [128, 2, DS, TC], F16)
    s_t = S("s_t", [128, 2, DS, TC], F16)
    dt_f = S("dt_f", [128, 2, TC], F32)
    a_f = S("a_f", [128, TC], F32)
    e_h = S("e_h", [128, TC], F16)
    l_h = S("l_h", [128, 2, TC], F16)
    r_f = S("r_f", [128, TC], F32)
    dtu = S("dtu", [128, 2, TC], F16)
    y_ssm = S("y_ssm", [128, 2, TC], F16)
    ysz = S("ysz", [128, TC], F16)
    g1 = S("g1", [128, TC], F16)
    s_prev = S("s_prev", [128, NDT, DS], F16)
    rowb = S("rowb", [1, 768], F32)
    rowb2 = S("rowb2", [1, 768], F32)
    cacc = S("cacc", [128, 2, TC], F32)
    fixt = S("fixt", [128, 2, DS], F16)

    pb = [es.enter_context(nc.psum_tensor(f"pb{i}", [128, 512], F32))
          for i in range(8)]

    sems = {k: es.enter_context(nc.semaphore(f"sem_{k}"))
            for k in ["pe", "act", "dve", "pool",
                      "dx", "dw", "dwi", "db", "do"]}

    stages = []
    DMA_ROLES = ("dx", "dw", "dwi", "db", "do")
    counts = {"pe": 0, "act": 0, "dve": 0, "pool": 0,
              "dx": 0, "dw": 0, "dwi": 0, "db": 0, "do": 0}
    writer = {}
    readers = {}

    last_role = {}
    STRICT = int(os.environ.get("MK_STRICT", "0"))
    region = ["main"]
    prev_tok = [None]

    def stage(eng, fn, reads=(), writes=(), ndma=1):
        is_dma = eng in DMA_ROLES
        deps = set()
        if is_dma and eng in last_role:
            deps.add(last_role[eng])
        RELAX = os.environ.get("MK_RELAX", "")
        if RELAX:
            strict_here = region[0] != RELAX
        else:
            strict_here = (region[0] == "pre" or STRICT == 1 or
                           (STRICT == 2 and region[0] != "scan") or
                           (STRICT == 3 and region[0] == "scan"))
        if strict_here and prev_tok[0] is not None:
            deps.add(prev_tok[0])
        for b in reads:
            if b in writer:
                deps.add(writer[b])
        for b in writes:
            if b in writer:
                deps.add(writer[b])
            for t in readers.get(b, ()):
                deps.add(t)
        if is_dma:
            counts[eng] += 16 * ndma
        else:
            counts[eng] += 1
        tok = (eng, counts[eng])
        best = {}
        for (e, v) in deps:
            if e == eng and not is_dma:
                continue
            if e not in best or v > best[e]:
                best[e] = v
        waits = sorted(best.items())
        stages.append((eng, fn, waits, ndma))
        prev_tok[0] = tok
        if is_dma:
            last_role[eng] = tok
        for b in reads:
            readers.setdefault(b, []).append(tok)
        for b in writes:
            writer[b] = tok
            readers[b] = []
        return tok

    region[0] = 'pre'
    def d_loads(h):
        h.dma_start(out=acol[:], in_=alog_c)
        h.dma_start(out=rows[:], in_=rows_c)
    stage("dw", d_loads, reads=(), writes=("acol", "rows"), ndma=2)

    stage("act", lambda h: h.activation(out=acol[:], in_=acol[:], func=AF.Exp),
          reads=("acol",), writes=("acol",))

    def neg_and_halve(h):
        h.tensor_scalar(out=acol[:], in0=acol[:], scalar1=-1.0, scalar2=None,
                        op0=OP.mult)
        return h.tensor_scalar(out=rows[:, COL_SC:COL_BI + 8],
                               in0=rows[:, COL_SC:COL_BI + 8],
                               scalar1=0.5, scalar2=None, op0=OP.mult)
    stage("dve", neg_and_halve, reads=("acol", "rows"), writes=("acol", "rows"))

    def mk_consts(h):
        h.memset(ones16[:], 1.0)
        h.memset(ones32r[:], 1.0)
        h.memset(s_prev[:], 0.0)
        return h.memset(x_inner[:, :, 0:3], 0.0)
    stage("dve", mk_consts, writes=("ones", "sprev", "xi"))

    stage("dw", lambda h: h.dma_start(out=stage_w[0:DR, :], in_=w_dtT),
          writes=("stg",))
    stage("dve", lambda h: h.tensor_copy(out=wdtT[:], in_=stage_w[0:DR, :]),
          reads=("x0",), writes=("wdtT",))

    wx_r = w_xT.rearrange("(k p) e -> p k e", p=128)
    for j in range(0, NDT, 8):
        stage("dw", lambda h, j=j: h.dma_start(out=stage_w[:, 0:768],
                                               in_=wx_r[:, j:j + 8, :]),
              writes=("stg",))
        stage("dve", lambda h, j=j: h.tensor_copy(
            out=wxT[:, j:j + 8, :],
            in_=stage_w[:, 0:768].rearrange("p (k e) -> p k e", k=8)),
            reads=("x0",), writes=("wxT",))

    wo_r = w_outT.rearrange("(k p) e -> p k e", p=128)
    wo16_r = w_out16.rearrange("(k p) e -> p k e", p=128)
    for j in range(0, NDT, 2):
        stage("dw", lambda h, j=j: h.dma_start(out=stage_w,
                                               in_=wo_r[:, j:j + 2, :]),
              writes=("x0",))
        stage("dve", lambda h, j=j: h.tensor_copy(
            out=wob[:], in_=stage_w.rearrange("p (k e) -> p k e", k=2)),
            reads=("x0",), writes=("wob0", "wob1"))
        stage("dw", lambda h, j=j: h.dma_start(
            out=wo16_r[:, j:j + 2, :], in_=wob[:]),
            reads=("wob0", "wob1"), writes=("wo16",))

    wf_r = w_ffnT.rearrange("(k p) e -> p k e", p=128)
    for j in range(0, NKT, 2):
        stage("dw", lambda h, j=j: h.dma_start(out=stage_w,
                                               in_=wf_r[:, j:j + 2, :]),
              writes=("x0",))

        def wf_sgn(h, j=j):
            sv = stage_w.rearrange("p (k e) -> p k e", k=2)
            h.tensor_scalar(out=sgnT[:, j:j + 2, :], in0=sv, scalar1=0.0,
                            scalar2=None, op0=OP.is_ge)
            return h.tensor_scalar(out=sgnT[:, j:j + 2, :],
                                   in0=sgnT[:, j:j + 2, :], scalar1=2.0,
                                   scalar2=-1.0, op0=OP.mult, op1=OP.add)
        stage("dve", wf_sgn, reads=("x0",), writes=("sgnT",))

    wi_r = w_inT.rearrange("(k p) e -> p k e", p=128)

    for kt in range(NKT):
        for hh in range(4):
            stage("dw", lambda h, kt=kt, hh=hh: h.dma_start(
                out=stage_w[:, 0:1024], in_=wi_r[:, kt, hh * 1024:(hh + 1) * 1024]),
                writes=("x0",))
            stage("dve", lambda h: h.tensor_copy(out=winb[:, 0, :],
                                                 in_=stage_w[:, 0:1024]),
                  reads=("x0",), writes=("winb0",))
            stage("dw", lambda h, kt=kt, hh=hh: h.dma_start(
                out=w_in16[kt * 4 + hh],
                in_=winb[:, 0, :]),
                reads=("winb0",), writes=("w16",))

    region[0] = 'main'
    def emit_ln(cp, rb):
        _saved = region[0]; region[0] = 'ffnln'
        xv, bv, qv, nv = x_sb[:, cp], xbf[:], xq[:], xn[:, cp]
        kx, kb, kq, kn = f"x{cp}", "xbf", "xq", f"xn{cp}"
        rbk = "rowb" if rb is rowb else "rowb2"

        xq_keys = tuple(f"{kq}_{mt}" for mt in range(NKT))

        def cvt(h):
            h.tensor_copy(out=bv, in_=xv)
            return h.tensor_tensor(out=qv, in0=bv, in1=bv, op=OP.mult)
        stage("dve", cvt, reads=(kx,), writes=(kb,) + xq_keys)

        def lstats(h):
            for kt in range(NKT):
                h.matmul(out=pb[4][0:1, 0:TC], lhsT=ones16[:], rhs=bv[:, kt, :],
                         start=(kt == 0), stop=(kt == NKT - 1))
            ins = None
            for kt in range(NKT):
                ins = h.matmul(out=pb[4][0:1, TC:2 * TC], lhsT=ones16[:],
                               rhs=qv[:, kt, :],
                               start=(kt == 0), stop=(kt == NKT - 1))
            return ins
        stage("pe", lstats, reads=(kb, "ones") + xq_keys, writes=("pb4",))

        def rowmath(h, rb=rb):
            h.tensor_scalar(out=rb[0:1, 0:TC], in0=pb[4][0:1, 0:TC],
                            scalar1=1.0 / DM, scalar2=None, op0=OP.mult)
            h.tensor_scalar(out=rb[0:1, 512:512 + TC], in0=pb[4][0:1, TC:2 * TC],
                            scalar1=1.0 / DM, scalar2=None, op0=OP.mult)
            h.tensor_tensor(out=rb[0:1, 256:256 + TC], in0=rb[0:1, 0:TC],
                            in1=rb[0:1, 0:TC], op=OP.mult)
            h.tensor_tensor(out=rb[0:1, 256:256 + TC],
                            in0=rb[0:1, 512:512 + TC],
                            in1=rb[0:1, 256:256 + TC], op=OP.subtract)
            return h.tensor_scalar(out=rb[0:1, 256:256 + TC],
                                   in0=rb[0:1, 256:256 + TC],
                                   scalar1=EPS, scalar2=None, op0=OP.add)
        stage("dve", rowmath, reads=("pb4",), writes=(rbk,))

        stage("act", lambda h, rb=rb: h.activation(out=rb[0:1, 256:256 + TC],
                                                   in_=rb[0:1, 256:256 + TC],
                                                   func=AF.Ln),
              reads=(rbk,), writes=(rbk,))
        stage("act", lambda h, rb=rb: h.activation(out=rb[0:1, 256:256 + TC],
                                                   in_=rb[0:1, 256:256 + TC],
                                                   func=AF.Exp, scale=-0.5),
              reads=(rbk,), writes=(rbk,))
        stage("dve", lambda h, rb=rb: h.tensor_tensor(out=rb[0:1, 512:512 + TC],
                                                      in0=rb[0:1, 0:TC],
                                                      in1=rb[0:1, 256:256 + TC],
                                                      op=OP.mult),
              reads=(rbk,), writes=(rbk,))

        def reps(h, rb=rb):
            h.matmul(out=pb[5][:, 0:TC], lhsT=ones32r[:],
                     rhs=rb[0:1, 256:256 + TC], start=True, stop=True)
            return h.matmul(out=pb[5][:, TC:2 * TC], lhsT=ones32r[:],
                            rhs=rb[0:1, 512:512 + TC], start=True, stop=True)
        stage("pe", reps, reads=(rbk, "ones"), writes=("pb5",))

        def apply(h):
            ins = None
            for kt in range(NKT):
                h.tensor_tensor(out=nv[:, kt, :], in0=xv[:, kt, :],
                                in1=pb[5][:, 0:TC], op=OP.mult)
                ins = h.tensor_tensor(out=nv[:, kt, :], in0=nv[:, kt, :],
                                      in1=pb[5][:, TC:2 * TC], op=OP.subtract)
            return ins
        stage("dve", apply, reads=(kx, "pb5"), writes=(kn,))
        region[0] = _saved

    def ffn_mt(cp, rb, mt):
        _sv = region[0]; region[0] = 'ffnln'
        xv, qv, nv = x_sb[:, cp], xq[:], xn[:, cp]
        kx, kn = f"x{cp}", f"xn{cp}"
        bk = f"pb{4 + mt % 2}"

        def mm(h, mt=mt):
            ins = None
            for kt in range(NKT):
                ins = h.matmul(out=pb[4 + mt % 2][:, 0:TC],
                               lhsT=sgnT[:, kt, mt * 128:(mt + 1) * 128],
                               rhs=nv[:, kt, :],
                               start=(kt == 0), stop=(kt == NKT - 1))
            return ins
        stage("pe", mm, reads=(kn, "sgnT"), writes=(bk,))

        stage("act", lambda h, mt=mt: h.activation(
            out=qv[:, mt, :], in_=pb[4 + mt % 2][:, 0:TC], func=AF.Identity,
            scale=rows[:, COL_SC + mt:COL_SC + mt + 1],
            bias=rows[:, COL_BI + mt:COL_BI + mt + 1]),
            reads=(bk, "rows"), writes=(f"xq_{mt}",))

        def clipadd(h, mt=mt):
            h.tensor_scalar(out=qv[:, mt, :], in0=qv[:, mt, :],
                            scalar1=-50.0, scalar2=50.0,
                            op0=OP.max, op1=OP.min)
            return h.tensor_tensor(out=xv[:, mt, :], in0=xv[:, mt, :],
                                   in1=qv[:, mt, :], op=OP.add)
        stage("dve", clipadd, reads=(f"xq_{mt}", kx), writes=(kx,))
        region[0] = _sv

    def ffn_thunks(cp, rb):
        return ([lambda: emit_ln(cp, rb)] +
                [lambda mt=mt: ffn_mt(cp, rb, mt) for mt in range(NKT)])

    def emit_ffn(cp, rb):
        for t in ffn_thunks(cp, rb):
            t()

    xT_r = xT.rearrange("(k p) t -> p k t", p=128)
    out_r = out_fm.rearrange("(k p) t -> p k t", p=128)

    def emit_load(c):
        cp = c % 2
        stage("dx", lambda h, c=c, cp=cp: h.dma_start(
            out=x_sb[:, cp], in_=xT_r[:, :, c * TC:(c + 1) * TC]),
            writes=(f"x{cp}",))

    def pre_thunks(c, with_load=True):
        cp = c % 2
        kx, kn = f"x{cp}", f"xn{cp}"
        th = []
        if with_load:
            th.append(lambda: emit_load(c))
        if c > 0:
            th.append(lambda: stage(
                "dve", lambda h: h.tensor_copy(out=x_inner[:, :, 0:3],
                                               in_=x_inner[:, :, TC:TC + 3]),
                reads=("xi",), writes=("xi",)))
        th += ffn_thunks(cp, rowb)
        th.append(lambda: emit_ln(cp, rowb))

        def win_kt(hoff, kt, cp=cp, kn=kn):
            if kt == 0:
                stage("dwi", lambda h, hoff=hoff: h.dma_start(
                    out=winb[:, 0, :], in_=w_in16[hoff // 1024]),
                    reads=("w16",), writes=("winb0",))
            if kt + 1 < NKT:
                q = (kt + 1) % 2
                stage("dwi", lambda h, kt=kt, hoff=hoff, q=q: h.dma_start(
                    out=winb[:, q, :],
                    in_=w_in16[(kt + 1) * 4 + hoff // 1024]),
                    reads=("w16",), writes=(f"winb{q}",))

            def win_mm(h, kt=kt, cp=cp):
                wt = winb[:, kt % 2, :]
                ins = None
                for mi in range(8):
                    ins = h.matmul(
                        out=pb[mi // 2][:, (mi % 2) * TC:(mi % 2 + 1) * TC],
                        lhsT=wt[:, mi * 128:(mi + 1) * 128],
                        rhs=xn[:, cp, kt, :],
                        start=(kt == 0), stop=(kt == NKT - 1))
                return ins
            stage("pe", win_mm, reads=(kn, f"winb{kt % 2}"),
                  writes=("pb0", "pb1", "pb2", "pb3"))

        def win_evac(half, pas, cp=cp):
            mt0 = half * 16 + pas * 8
            if half == 0:
                def xi_evac(h, mt0=mt0):
                    ins = None
                    for mi in range(8):
                        ins = h.activation(
                            out=x_inner[:, mt0 + mi, 3:3 + TC],
                            in_=pb[mi // 2][:, (mi % 2) * TC:(mi % 2 + 1) * TC],
                            func=AF.Copy)
                    return ins
                stage("act", xi_evac, reads=("pb0", "pb1", "pb2", "pb3"),
                      writes=("xi",))
            else:
                def z_silu(h, mt0=mt0, cp=cp):
                    ins = None
                    for mi in range(8):
                        ins = h.activation(
                            out=sz[:, cp, mt0 - 16 + mi, :],
                            in_=pb[mi // 2][:, (mi % 2) * TC:(mi % 2 + 1) * TC],
                            func=AF.Silu)
                    return ins
                stage("act", z_silu, reads=("pb0", "pb1", "pb2", "pb3"),
                      writes=(f"sz{cp}",))

        for half in range(2):
            for pas in range(2):
                hoff = half * 2048 + pas * 1024
                for kt in range(NKT):
                    th.append(lambda hoff=hoff, kt=kt: win_kt(hoff, kt))
                th.append(lambda half=half, pas=pas: win_evac(half, pas))

        def conv_group(d0, cp=cp):
            def conv_mac(h, d0=d0, cp=cp):
                ins = None
                for dt in range(d0, d0 + 4):
                    xiv = x_inner[:, dt, :]
                    h.tensor_scalar(out=cacc[:, 0], in0=xiv[:, 0:TC],
                                    scalar1=rows[:, COL_CW + dt * 4:COL_CW + dt * 4 + 1],
                                    scalar2=rows[:, COL_CB + dt:COL_CB + dt + 1],
                                    op0=OP.mult, op1=OP.add)
                    h.scalar_tensor_tensor(
                        out=cacc[:, 1], in0=xiv[:, 1:1 + TC],
                        scalar=rows[:, COL_CW + dt * 4 + 1:COL_CW + dt * 4 + 2],
                        in1=cacc[:, 0], op0=OP.mult, op1=OP.add)
                    h.scalar_tensor_tensor(
                        out=cacc[:, 0], in0=xiv[:, 2:2 + TC],
                        scalar=rows[:, COL_CW + dt * 4 + 2:COL_CW + dt * 4 + 3],
                        in1=cacc[:, 1], op0=OP.mult, op1=OP.add)
                    ins = h.scalar_tensor_tensor(
                        out=x_conv[:, cp, dt, :], in0=xiv[:, 3:3 + TC],
                        scalar=rows[:, COL_CW + dt * 4 + 3:COL_CW + dt * 4 + 4],
                        in1=cacc[:, 0], op0=OP.mult, op1=OP.add)
                return ins
            stage("dve", conv_mac, reads=("xi", "rows", "cacc"),
                  writes=(f"xc{cp}", "cacc"))

        def conv_silu(d0, cp=cp):
            def c_silu(h, d0=d0, cp=cp):
                ins = None
                for dt in range(d0, d0 + 4):
                    ins = h.activation(out=x_conv[:, cp, dt, :],
                                       in_=x_conv[:, cp, dt, :], func=AF.Silu)
                return ins
            stage("act", c_silu, reads=(f"xc{cp}",), writes=(f"xc{cp}",))

        for d0 in range(0, NDT, 4):
            th.append(lambda d0=d0: conv_group(d0))
            th.append(lambda d0=d0: conv_silu(d0))

        def wx_all(cp=cp):
            def wx_mm(h, cp=cp):
                ins = None
                for kt in range(NDT):
                    ins = h.matmul(out=pb[4][0:96, TC:2 * TC], lhsT=wxT[:, kt, :],
                                   rhs=x_conv[:, cp, kt, :],
                                   start=(kt == 0), stop=(kt == NDT - 1))
                return ins
            stage("pe", wx_mm, reads=(f"xc{cp}", "wxT"), writes=("pb4",))
            stage("act", lambda h, cp=cp: h.activation(
                out=dtr[:, cp, :], in_=pb[4][0:64, TC:2 * TC], func=AF.Copy),
                reads=("pb4",), writes=(f"dtr{cp}",))
            stage("act", lambda h: h.activation(
                out=bc_sb[:], in_=pb[4][64:96, TC:2 * TC], func=AF.Copy),
                reads=("pb4",), writes=("bcs",))
            stage("db", lambda h: h.dma_start(out=bcflat, in_=bc_sb[:]),
                  reads=("bcs",), writes=("bcf",))

            def bc_bcast(h, cp=cp):
                h.dma_start(out=Brep[:, cp],
                            in_=bcflat[0:16, :].partition_broadcast(128))
                h.dma_start(out=Crep[:, cp],
                            in_=bcflat[16:32, :].partition_broadcast(128))
            stage("db", bc_bcast, reads=("bcf",),
                  writes=(f"Brep{cp}", f"Crep{cp}"), ndma=2)
        th.append(wx_all)
        return th

    dt0_pref = [False]

    def emit_scan(c, post, pre):
        cp = c % 2
        region[0] = "scan"
        q = list(post) + list(pre)
        done = [0]

        def drain(upto):
            upto = min(upto, len(q))
            while done[0] < upto:
                q[done[0]]()
                done[0] += 1

        def dt_mm_act(dt, cpd=cp):
            p = dt % 2
            bk = f"pb{6 + p}"
            stage("pe", lambda h, dt=dt, p=p, cpd=cpd: h.matmul(
                out=pb[6 + p][:, 0:TC], lhsT=wdtT[:, dt * 128:(dt + 1) * 128],
                rhs=dtr[:, cpd, :], start=True, stop=True),
                reads=(f"dtr{cpd}", "wdtT"), writes=(bk,))

            def act1(h, dt=dt, p=p):
                h.activation(out=a_f[:], in_=pb[6 + p][:, 0:TC], func=AF.Abs,
                             bias=rows[:, COL_BDT + dt:COL_BDT + dt + 1])
                h.activation(out=e_h[:], in_=a_f[:], func=AF.Exp, scale=-1.0)
                return h.activation(out=l_h[:, p], in_=e_h[:], func=AF.Ln,
                                    bias=1.0)
            stage("act", act1, reads=(bk, "rows", "af", "eh"),
                  writes=("af", "eh", f"lh{p}"))

        def dt_fin(dt, cpd=cp):
            p = dt % 2
            bk = f"pb{6 + p}"

            def sp_fin(h, dt=dt, p=p, cpd=cpd):
                h.tensor_scalar(out=r_f[:], in0=pb[6 + p][:, 0:TC],
                                scalar1=rows[:, COL_BDT + dt:COL_BDT + dt + 1],
                                scalar2=0.0, op0=OP.add, op1=OP.max)
                h.tensor_tensor(out=dt_f[:, p], in0=r_f[:], in1=l_h[:, p],
                                op=OP.add)
                return h.tensor_tensor(out=dtu[:, p], in0=dt_f[:, p],
                                       in1=x_conv[:, cpd, dt, :], op=OP.mult)
            stage("dve", sp_fin, reads=(bk, f"lh{p}", f"xc{cpd}", "rows", "rf"),
                  writes=("rf", f"dtf{p}", f"dtu{p}"))

            def mk_dA(h, dt=dt, p=p):
                ins = None
                for n in range(DS):
                    col = dt * DS + n
                    ins = h.activation(out=dA[:, p, n, :], in_=dt_f[:, p],
                                       func=AF.Exp, scale=acol[:, col:col + 1])
                return ins
            stage("act", mk_dA, reads=(f"dtf{p}", "acol"), writes=(f"dA{p}",))

        def dt_path(dt, cpd=cp):
            dt_mm_act(dt, cpd)
            dt_fin(dt, cpd)

        def scan_block(dt):
            p = dt % 2

            def scan_pre(h, dt=dt, p=p):
                h.tensor_tensor(out=fixt[:, p].unsqueeze(2),
                                in0=dA[:, p, :, 0:1],
                                in1=s_prev[:, dt, :].unsqueeze(2), op=OP.mult)
                h.tensor_tensor(out=w_t[:, p],
                                in0=dtu[:, p].unsqueeze(1)
                                    .broadcast_to([128, DS, TC]),
                                in1=Brep[:, cp], op=OP.mult)
                h.tensor_tensor(out=w_t[:, p, :, 0:1], in0=w_t[:, p, :, 0:1],
                                in1=fixt[:, p].unsqueeze(2), op=OP.add)
                h.memset(dA[:, p, :, 0:1], 0.0)
                h.tensor_tensor_scan(out=s_t[:, p].rearrange("p n t -> p (n t)"),
                                     data0=dA[:, p].rearrange("p n t -> p (n t)"),
                                     data1=w_t[:, p].rearrange("p n t -> p (n t)"),
                                     initial=0.0, op0=OP.mult, op1=OP.add)
                return h.tensor_copy(out=s_prev[:, dt, :].unsqueeze(2),
                                     in_=s_t[:, p, :, TC - 1:TC])
            stage("dve", scan_pre,
                  reads=(f"dA{p}", f"dtu{p}", f"Brep{cp}", "sprev"),
                  writes=(f"wt{p}", f"st{p}", "sprev", f"dA{p}", f"fx{p}"))

        def scan_drain(dt):
            p = dt % 2

            def nred(h, dt=dt, p=p):
                fl = lambda ap: ap.rearrange("p n t -> p (n t)")
                h.tensor_tensor(out=fl(w_t[:, p]), in0=fl(s_t[:, p]),
                                in1=fl(Crep[:, cp]), op=OP.mult)
                h.tensor_tensor(out=fl(s_t[:, p, 0:8, :]),
                                in0=fl(w_t[:, p, 0:8, :]),
                                in1=fl(w_t[:, p, 8:16, :]), op=OP.add)
                h.tensor_tensor(out=fl(s_t[:, p, 8:12, :]),
                                in0=fl(s_t[:, p, 0:4, :]),
                                in1=fl(s_t[:, p, 4:8, :]), op=OP.add)
                h.tensor_tensor(out=fl(s_t[:, p, 12:14, :]),
                                in0=fl(s_t[:, p, 8:10, :]),
                                in1=fl(s_t[:, p, 10:12, :]), op=OP.add)
                h.tensor_tensor(out=y_ssm[:, p], in0=s_t[:, p, 12, :],
                                in1=s_t[:, p, 13, :], op=OP.add)
                h.tensor_tensor(out=ysz[:], in0=y_ssm[:, p],
                                in1=sz[:, cp, dt, :], op=OP.mult)
                return h.scalar_tensor_tensor(
                    out=ygate[:, dt, :], in0=x_conv[:, cp, dt, :],
                    scalar=rows[:, COL_D + dt:COL_D + dt + 1], in1=ysz[:],
                    op0=OP.mult, op1=OP.add)
            stage("dve", nred,
                  reads=(f"st{p}", f"Crep{cp}", f"sz{cp}", f"xc{cp}", "rows"),
                  writes=(f"wt{p}", f"st{p}", f"yssm{p}", f"yg{dt}"))

        if dt0_pref[0]:
            dt0_pref[0] = False
        else:
            dt_path(0)
        for k in range(NDT):
            if k + 1 < NDT:
                dt_mm_act(k + 1)
            drain(-(-len(q) * (k + 1) // (NDT - 2)))
            scan_block(k)
            if k >= 1:
                scan_drain(k - 1)
            if k + 1 < NDT:
                dt_fin(k + 1)
            if k == NDT - 2 and pre:
                dt_path(0, cpd=1 - cp)
                dt0_pref[0] = True
        scan_drain(NDT - 1)
        drain(len(q))
        region[0] = "main"

    def post_thunks(c):
        cp = c % 2
        kx = f"x{cp}"
        th = []

        def wout_kt(kt):
            if kt == 0:
                stage("dw", lambda h: h.dma_start(out=wob[:, 0, :],
                                                  in_=wo16_r[:, 0, :]),
                      reads=("wo16",), writes=("wob0",))
            if kt + 1 < NDT:
                q = (kt + 1) % 2
                stage("dw", lambda h, kt=kt, q=q: h.dma_start(
                    out=wob[:, q, :], in_=wo16_r[:, kt + 1, :]),
                    reads=("wo16",), writes=(f"wob{q}",))

            def wout_mm(h, kt=kt):
                ins = None
                for mt in range(NKT):
                    ins = h.matmul(
                        out=pb[mt // 2][:, (mt % 2) * TC:(mt % 2 + 1) * TC],
                        lhsT=wob[:, kt % 2, mt * 128:(mt + 1) * 128],
                        rhs=ygate[:, kt, :],
                        start=(kt == 0), stop=(kt == NDT - 1))
                return ins
            stage("pe", wout_mm, reads=(f"yg{kt}", f"wob{kt % 2}"),
                  writes=tuple(f"pb{i}" for i in range(4)))
        for kt in range(NDT):
            th.append(lambda kt=kt: wout_kt(kt))

        def wout_add_t():
            def wout_add(h, cp=cp):
                ins = None
                for mt in range(NKT):
                    ins = h.tensor_tensor(
                        out=x_sb[:, cp, mt, :], in0=x_sb[:, cp, mt, :],
                        in1=pb[mt // 2][:, (mt % 2) * TC:(mt % 2 + 1) * TC],
                        op=OP.add)
                return ins
            stage("dve", wout_add,
                  reads=tuple(f"pb{i}" for i in range(4)) + (kx,), writes=(kx,))
        th.append(wout_add_t)
        th += ffn_thunks(cp, rowb2)
        th.append(lambda: stage("do", lambda h, c=c, cp=cp: h.dma_start(
            out=out_r[:, :, c * TC:(c + 1) * TC], in_=x_sb[:, cp]),
            reads=(kx,), writes=()))
        return th

    emit_load(0)
    for t in pre_thunks(0, with_load=False):
        t()
    post_q = []
    for c in range(NCH):
        pre = []
        if c + 1 < NCH:
            pre = pre_thunks(c + 1)
        emit_scan(c, post_q, pre)
        post_q = post_thunks(c)
    for t in post_q:
        t()

    final_waits = [(r, counts[r]) for r in DMA_ROLES if counts[r] > 0]
    stages.append(("dve", lambda h: h.memset(fixt[:, 0], 0.0),
                   final_waits, 1))
    counts["dve"] += 1

    DMA_SET = set(DMA_ROLES)

    def emit_for(ename, h):
        waited = {}
        for eng, fn, waits, ndma in stages:
            sel = "dma" if eng in DMA_SET else eng
            if sel != ename:
                continue
            for (wsem, wval) in waits:
                if waited.get(wsem, -1) >= wval:
                    continue
                h.wait_ge(sems[wsem], wval)
                waited[wsem] = wval
            if sel == "dma":
                proxy = DmaProxy(h, sems[eng])
                fn(proxy)
                assert proxy.n == ndma, f"dma stage issued {proxy.n} != {ndma}"
            else:
                ins = fn(h)
                assert ins is not None, "compute stage returned no instruction"
                ins.then_inc(sems[eng], 1)

    with es, nc.Block() as block:
        @block.sync
        def _(sync):
            emit_for("dma", sync)

        @block.tensor
        def _(tensor):
            emit_for("pe", tensor)

        @block.scalar
        def _(scalar):
            emit_for("act", scalar)

        @block.vector
        def _(vector):
            emit_for("dve", vector)

        @block.gpsimd
        def _(gpsimd):
            emit_for("pool", gpsimd)

    return nc


_NC_CACHE = None


def kernel(L0_attn_x, L0_attn_W_in, L0_attn_conv_w, L0_attn_conv_b,
           L0_attn_W_x, L0_attn_W_dt, L0_attn_b_dt, L0_attn_A_log, L0_attn_D,
           L0_attn_W_out, L0_ffn_weights, L0_ffn_scale, L0_ffn_bias):
    global _NC_CACHE
    x = np.ascontiguousarray(np.asarray(L0_attn_x, np.float32))

    def colpack(v, k):
        return np.ascontiguousarray(np.asarray(v, np.float32).reshape(k, 128).T)

    rows_c = np.zeros((128, 128), np.float32)
    rows_c[:, COL_D:COL_D + 16] = colpack(L0_attn_D, 16)
    rows_c[:, COL_CB:COL_CB + 16] = colpack(L0_attn_conv_b, 16)
    rows_c[:, COL_BDT:COL_BDT + 16] = colpack(L0_attn_b_dt, 16)
    cw = np.asarray(L0_attn_conv_w, np.float32)[:, 0, :]
    rows_c[:, COL_CW:COL_CW + 64] = cw.reshape(16, 128, 4).transpose(1, 0, 2) \
        .reshape(128, 64)
    rows_c[:, COL_SC:COL_SC + 8] = colpack(L0_ffn_scale, 8)
    rows_c[:, COL_BI:COL_BI + 8] = colpack(L0_ffn_bias, 8)

    alog_c = np.ascontiguousarray(
        np.asarray(L0_attn_A_log, np.float32).reshape(16, 128, 16)
        .transpose(1, 0, 2).reshape(128, 256))

    w_inT = np.ascontiguousarray(np.asarray(L0_attn_W_in, np.float32).T)
    w_outT = np.ascontiguousarray(np.asarray(L0_attn_W_out, np.float32).T)
    w_ffnT = np.ascontiguousarray(np.asarray(L0_ffn_weights, np.float32).T)
    w_xT = np.ascontiguousarray(np.asarray(L0_attn_W_x, np.float32).T)
    w_dtT = np.ascontiguousarray(np.asarray(L0_attn_W_dt, np.float32).T)

    if _NC_CACHE is None:
        _NC_CACHE = build_nc()
    nc = _NC_CACHE

    shared = {"w_inT": w_inT, "w_outT": w_outT, "w_ffnT": w_ffnT,
              "w_xT": w_xT, "w_dtT": w_dtT, "alog_c": alog_c, "rows_c": rows_c}
    in_maps = []
    for b in range(8):
        m = dict(shared)
        m["xT"] = np.ascontiguousarray(x[b].T)
        in_maps.append(m)

    res = run_bass_kernel_spmd(nc, in_maps, core_ids=list(range(8)))
    out = np.empty((8, T, DM), np.float32)
    for b in range(8):
        out[b] = np.asarray(res.results[b]["out_fm"]).T
    return out


# version 23
# speedup vs baseline: 1.1701x; 1.1701x over previous
import os
import numpy as np
import ml_dtypes
import concourse.bass as bass
import concourse.mybir as mybir
from concourse.bass_utils import run_bass_kernel_spmd

F32 = mybir.dt.float32
F16 = mybir.dt.float16
AF = mybir.ActivationFunctionType
OP = mybir.AluOpType

T = 2048
DM = 1024
DI = 2048
DS = 16
DR = 64
TC = 256
NCH = T // TC
NKT = DM // 128
NDT = DI // 128
EPS = 1e-5

COL_D, COL_CB, COL_BDT, COL_CW = 0, 16, 32, 48
COL_SC, COL_BI = 112, 120


class DmaProxy:
    def __init__(self, h, dsem):
        self.h = h
        self.dsem = dsem
        self.n = 0

    def dma_start(self, **kw):
        ins = self.h.dma_start(**kw)
        ins.then_inc(self.dsem, 16)
        self.n += 1
        return ins


def build_nc():
    nc = bass.Bass("TRN2", target_bir_lowering=False,
                   dynamic_dma_scratch_size=1024)

    xT = nc.dram_tensor("xT", [DM, T], F32, kind="ExternalInput").ap()
    w_inT = nc.dram_tensor("w_inT", [DM, 2 * DI], F32, kind="ExternalInput").ap()
    w_outT = nc.dram_tensor("w_outT", [DI, DM], F32, kind="ExternalInput").ap()
    w_ffnT = nc.dram_tensor("w_ffnT", [DM, DM], F32, kind="ExternalInput").ap()
    w_xT = nc.dram_tensor("w_xT", [DI, 96], F32, kind="ExternalInput").ap()
    w_dtT = nc.dram_tensor("w_dtT", [DR, DI], F32, kind="ExternalInput").ap()
    alog_c = nc.dram_tensor("alog_c", [128, NDT * DS], F32, kind="ExternalInput").ap()
    rows_c = nc.dram_tensor("rows_c", [128, 128], F32, kind="ExternalInput").ap()
    out_fm = nc.dram_tensor("out_fm", [DM, T], F32, kind="ExternalOutput").ap()

    MK_DEBUG = int(os.environ.get("MK_DEBUG", "0"))
    dbg = {}
    if MK_DEBUG:
        for nm, shape, dt in [("dbg_x1", [DM, TC], F32),
                              ("dbg_xc", [DI, TC], F16),
                              ("dbg_sz", [DI, TC], F16),
                              ("dbg_yg", [DI, TC], F16),
                              ("dbg_xn", [DM, TC], F16),
                              ("dbg_xi", [DI, TC + 3], F16)]:
            dbg[nm] = nc.dram_tensor(nm, shape, dt, kind="ExternalOutput").ap()
    bcflat = nc.dram_tensor("bcflat", [32, TC], F16, kind="Internal").ap()
    w_in16 = nc.dram_tensor("w_in16", [4 * NKT, 128, 1024], F16, kind="Internal").ap()
    w_out16 = nc.dram_tensor("w_out16", [DI, DM], F16, kind="Internal").ap()

    from contextlib import ExitStack
    es = ExitStack()
    S = lambda name, shape, dt: es.enter_context(nc.sbuf_tensor(name, shape, dt))

    winb = S("winb", [128, 2, 1024], F16)
    wob = S("wob", [128, 2, DM], F16)
    sgnT = S("sgnT", [128, NKT, DM], F16)
    wxT = S("wxT", [128, NDT, 96], F16)
    wdtT = S("wdtT", [DR, DI], F16)
    acol = S("acol", [128, NDT * DS], F32)
    rows = S("rows", [128, 128], F32)
    ones16 = S("ones16", [128, 1], F16)
    ones32r = S("ones32r", [1, 128], F32)

    x_sb = S("x_sb", [128, 2, NKT, TC], F32)
    stage_w = x_sb[:, 0].rearrange("p k t -> p (k t)")
    xbf = S("xbf", [128, NKT, TC], F16)
    xq = S("xq", [128, NKT, TC], F16)
    xn = S("xn", [128, 2, NKT, TC], F16)
    x_inner = S("x_inner", [128, NDT, TC + 3], F16)
    x_conv = S("x_conv", [128, 2, NDT, TC], F16)
    sz = S("sz", [128, 2, NDT, TC], F16)
    dtr = S("dtr", [DR, 2, TC], F16)
    bc_sb = S("bc_sb", [32, TC], F16)
    Brep = S("Brep", [128, 2, DS, TC], F16)
    Crep = S("Crep", [128, 2, DS, TC], F16)
    ygate = S("ygate", [128, NDT, TC], F16)
    dA = S("dA", [128, 2, DS, TC], F16)
    w_t = S("w_t", [128, 2, DS, TC], F16)
    s_t = S("s_t", [128, 2, DS, TC], F16)
    dt_f = S("dt_f", [128, 2, TC], F32)
    a_f = S("a_f", [128, TC], F32)
    e_h = S("e_h", [128, TC], F16)
    l_h = S("l_h", [128, 2, TC], F16)
    r_f = S("r_f", [128, TC], F32)
    dtu = S("dtu", [128, 2, TC], F16)
    y_ssm = S("y_ssm", [128, 2, TC], F16)
    ysz = S("ysz", [128, TC], F16)
    g1 = S("g1", [128, TC], F16)
    s_prev = S("s_prev", [128, NDT, DS], F16)
    rowb = S("rowb", [1, 768], F32)
    rowb2 = S("rowb2", [1, 768], F32)
    cacc = S("cacc", [128, 2, TC], F32)
    fixt = S("fixt", [128, 2, DS], F16)

    pb = [es.enter_context(nc.psum_tensor(f"pb{i}", [128, 512], F32))
          for i in range(8)]

    sems = {k: es.enter_context(nc.semaphore(f"sem_{k}"))
            for k in ["pe", "act", "dve", "pool",
                      "dx", "dw", "dwi", "db", "do"]}

    stages = []
    DMA_ROLES = ("dx", "dw", "dwi", "db", "do")
    counts = {"pe": 0, "act": 0, "dve": 0, "pool": 0,
              "dx": 0, "dw": 0, "dwi": 0, "db": 0, "do": 0}
    writer = {}
    readers = {}

    last_role = {}
    STRICT = int(os.environ.get("MK_STRICT", "0"))
    region = ["main"]
    prev_tok = [None]

    def stage(eng, fn, reads=(), writes=(), ndma=1):
        is_dma = eng in DMA_ROLES
        deps = set()
        if is_dma and eng in last_role:
            deps.add(last_role[eng])
        RELAX = os.environ.get("MK_RELAX", "")
        if RELAX:
            strict_here = region[0] != RELAX
        else:
            strict_here = (region[0] == "pre" or STRICT == 1 or
                           (STRICT == 2 and region[0] != "scan") or
                           (STRICT == 3 and region[0] == "scan"))
        if strict_here and prev_tok[0] is not None:
            deps.add(prev_tok[0])
        for b in reads:
            if b in writer:
                deps.add(writer[b])
        for b in writes:
            if b in writer:
                deps.add(writer[b])
            for t in readers.get(b, ()):
                deps.add(t)
        if is_dma:
            counts[eng] += 16 * ndma
        else:
            counts[eng] += 1
        tok = (eng, counts[eng])
        best = {}
        for (e, v) in deps:
            if e == eng and not is_dma:
                continue
            if e not in best or v > best[e]:
                best[e] = v
        waits = sorted(best.items())
        stages.append((eng, fn, waits, ndma))
        prev_tok[0] = tok
        if is_dma:
            last_role[eng] = tok
        for b in reads:
            readers.setdefault(b, []).append(tok)
        for b in writes:
            writer[b] = tok
            readers[b] = []
        return tok

    region[0] = 'pre'
    def d_loads(h):
        h.dma_start(out=acol[:], in_=alog_c)
        h.dma_start(out=rows[:], in_=rows_c)
    stage("dw", d_loads, reads=(), writes=("acol", "rows"), ndma=2)

    stage("act", lambda h: h.activation(out=acol[:], in_=acol[:], func=AF.Exp),
          reads=("acol",), writes=("acol",))

    def neg_and_halve(h):
        h.tensor_scalar(out=acol[:], in0=acol[:], scalar1=-1.0, scalar2=None,
                        op0=OP.mult)
        return h.tensor_scalar(out=rows[:, COL_SC:COL_BI + 8],
                               in0=rows[:, COL_SC:COL_BI + 8],
                               scalar1=0.5, scalar2=None, op0=OP.mult)
    stage("dve", neg_and_halve, reads=("acol", "rows"), writes=("acol", "rows"))

    def mk_consts(h):
        h.memset(ones16[:], 1.0)
        h.memset(ones32r[:], 1.0)
        h.memset(s_prev[:], 0.0)
        return h.memset(x_inner[:, :, 0:3], 0.0)
    stage("dve", mk_consts, writes=("ones", "sprev", "xi"))

    stage("dw", lambda h: h.dma_start(out=stage_w[0:DR, :], in_=w_dtT),
          writes=("stg",))
    stage("dve", lambda h: h.tensor_copy(out=wdtT[:], in_=stage_w[0:DR, :]),
          reads=("x0",), writes=("wdtT",))

    wx_r = w_xT.rearrange("(k p) e -> p k e", p=128)
    for j in range(0, NDT, 8):
        stage("dw", lambda h, j=j: h.dma_start(out=stage_w[:, 0:768],
                                               in_=wx_r[:, j:j + 8, :]),
              writes=("stg",))
        stage("dve", lambda h, j=j: h.tensor_copy(
            out=wxT[:, j:j + 8, :],
            in_=stage_w[:, 0:768].rearrange("p (k e) -> p k e", k=8)),
            reads=("x0",), writes=("wxT",))

    wo_r = w_outT.rearrange("(k p) e -> p k e", p=128)
    wo16_r = w_out16.rearrange("(k p) e -> p k e", p=128)
    for j in range(0, NDT, 2):
        stage("dw", lambda h, j=j: h.dma_start(out=stage_w,
                                               in_=wo_r[:, j:j + 2, :]),
              writes=("x0",))
        stage("dve", lambda h, j=j: h.tensor_copy(
            out=wob[:], in_=stage_w.rearrange("p (k e) -> p k e", k=2)),
            reads=("x0",), writes=("wob0", "wob1"))
        stage("dw", lambda h, j=j: h.dma_start(
            out=wo16_r[:, j:j + 2, :], in_=wob[:]),
            reads=("wob0", "wob1"), writes=("wo16",))

    wf_r = w_ffnT.rearrange("(k p) e -> p k e", p=128)
    for j in range(0, NKT, 2):
        stage("dw", lambda h, j=j: h.dma_start(out=stage_w,
                                               in_=wf_r[:, j:j + 2, :]),
              writes=("x0",))

        def wf_sgn(h, j=j):
            sv = stage_w.rearrange("p (k e) -> p k e", k=2)
            h.tensor_scalar(out=sgnT[:, j:j + 2, :], in0=sv, scalar1=0.0,
                            scalar2=None, op0=OP.is_ge)
            return h.tensor_scalar(out=sgnT[:, j:j + 2, :],
                                   in0=sgnT[:, j:j + 2, :], scalar1=2.0,
                                   scalar2=-1.0, op0=OP.mult, op1=OP.add)
        stage("dve", wf_sgn, reads=("x0",), writes=("sgnT",))

    wi_r = w_inT.rearrange("(k p) e -> p k e", p=128)

    for kt in range(NKT):
        for hh in range(4):
            stage("dw", lambda h, kt=kt, hh=hh: h.dma_start(
                out=stage_w[:, 0:1024], in_=wi_r[:, kt, hh * 1024:(hh + 1) * 1024]),
                writes=("x0",))
            stage("dve", lambda h: h.tensor_copy(out=winb[:, 0, :],
                                                 in_=stage_w[:, 0:1024]),
                  reads=("x0",), writes=("winb0",))
            stage("dw", lambda h, kt=kt, hh=hh: h.dma_start(
                out=w_in16[kt * 4 + hh],
                in_=winb[:, 0, :]),
                reads=("winb0",), writes=("w16",))

    region[0] = 'main'
    def emit_ln(cp, rb):
        _saved = region[0]; region[0] = 'ffnln'
        xv, bv, qv, nv = x_sb[:, cp], xbf[:], xq[:], xn[:, cp]
        kx, kb, kq, kn = f"x{cp}", "xbf", "xq", f"xn{cp}"
        rbk = "rowb" if rb is rowb else "rowb2"

        xq_keys = tuple(f"{kq}_{mt}" for mt in range(NKT))

        def cvt(h):
            h.tensor_copy(out=bv, in_=xv)
            return h.tensor_tensor(out=qv, in0=bv, in1=bv, op=OP.mult)
        stage("dve", cvt, reads=(kx,), writes=(kb,) + xq_keys)

        def lstats(h):
            for kt in range(NKT):
                h.matmul(out=pb[4][0:1, 0:TC], lhsT=ones16[:], rhs=bv[:, kt, :],
                         start=(kt == 0), stop=(kt == NKT - 1))
            ins = None
            for kt in range(NKT):
                ins = h.matmul(out=pb[4][0:1, TC:2 * TC], lhsT=ones16[:],
                               rhs=qv[:, kt, :],
                               start=(kt == 0), stop=(kt == NKT - 1))
            return ins
        stage("pe", lstats, reads=(kb, "ones") + xq_keys, writes=("pb4",))

        def rowmath(h, rb=rb):
            h.tensor_scalar(out=rb[0:1, 0:TC], in0=pb[4][0:1, 0:TC],
                            scalar1=1.0 / DM, scalar2=None, op0=OP.mult)
            h.tensor_scalar(out=rb[0:1, 512:512 + TC], in0=pb[4][0:1, TC:2 * TC],
                            scalar1=1.0 / DM, scalar2=None, op0=OP.mult)
            h.tensor_tensor(out=rb[0:1, 256:256 + TC], in0=rb[0:1, 0:TC],
                            in1=rb[0:1, 0:TC], op=OP.mult)
            h.tensor_tensor(out=rb[0:1, 256:256 + TC],
                            in0=rb[0:1, 512:512 + TC],
                            in1=rb[0:1, 256:256 + TC], op=OP.subtract)
            return h.tensor_scalar(out=rb[0:1, 256:256 + TC],
                                   in0=rb[0:1, 256:256 + TC],
                                   scalar1=EPS, scalar2=None, op0=OP.add)
        stage("dve", rowmath, reads=("pb4",), writes=(rbk,))

        stage("act", lambda h, rb=rb: h.activation(out=rb[0:1, 256:256 + TC],
                                                   in_=rb[0:1, 256:256 + TC],
                                                   func=AF.Ln),
              reads=(rbk,), writes=(rbk,))
        stage("act", lambda h, rb=rb: h.activation(out=rb[0:1, 256:256 + TC],
                                                   in_=rb[0:1, 256:256 + TC],
                                                   func=AF.Exp, scale=-0.5),
              reads=(rbk,), writes=(rbk,))
        stage("dve", lambda h, rb=rb: h.tensor_tensor(out=rb[0:1, 512:512 + TC],
                                                      in0=rb[0:1, 0:TC],
                                                      in1=rb[0:1, 256:256 + TC],
                                                      op=OP.mult),
              reads=(rbk,), writes=(rbk,))

        def reps(h, rb=rb):
            h.matmul(out=pb[5][:, 0:TC], lhsT=ones32r[:],
                     rhs=rb[0:1, 256:256 + TC], start=True, stop=True)
            return h.matmul(out=pb[5][:, TC:2 * TC], lhsT=ones32r[:],
                            rhs=rb[0:1, 512:512 + TC], start=True, stop=True)
        stage("pe", reps, reads=(rbk, "ones"), writes=("pb5",))

        def apply(h):
            ins = None
            for kt in range(NKT):
                h.tensor_tensor(out=nv[:, kt, :], in0=xv[:, kt, :],
                                in1=pb[5][:, 0:TC], op=OP.mult)
                ins = h.tensor_tensor(out=nv[:, kt, :], in0=nv[:, kt, :],
                                      in1=pb[5][:, TC:2 * TC], op=OP.subtract)
            return ins
        stage("dve", apply, reads=(kx, "pb5"), writes=(kn,))
        region[0] = _saved

    def ffn_mt(cp, rb, mt):
        _sv = region[0]; region[0] = 'ffnln'
        xv, qv, nv = x_sb[:, cp], xq[:], xn[:, cp]
        kx, kn = f"x{cp}", f"xn{cp}"
        bk = f"pb{4 + mt % 2}"

        def mm(h, mt=mt):
            ins = None
            for kt in range(NKT):
                ins = h.matmul(out=pb[4 + mt % 2][:, 0:TC],
                               lhsT=sgnT[:, kt, mt * 128:(mt + 1) * 128],
                               rhs=nv[:, kt, :],
                               start=(kt == 0), stop=(kt == NKT - 1))
            return ins
        stage("pe", mm, reads=(kn, "sgnT"), writes=(bk,))

        stage("act", lambda h, mt=mt: h.activation(
            out=qv[:, mt, :], in_=pb[4 + mt % 2][:, 0:TC], func=AF.Identity,
            scale=rows[:, COL_SC + mt:COL_SC + mt + 1],
            bias=rows[:, COL_BI + mt:COL_BI + mt + 1]),
            reads=(bk, "rows"), writes=(f"xq_{mt}",))

        def clipadd(h, mt=mt):
            h.tensor_scalar(out=qv[:, mt, :], in0=qv[:, mt, :],
                            scalar1=-50.0, scalar2=50.0,
                            op0=OP.max, op1=OP.min)
            return h.tensor_tensor(out=xv[:, mt, :], in0=xv[:, mt, :],
                                   in1=qv[:, mt, :], op=OP.add)
        stage("dve", clipadd, reads=(f"xq_{mt}", kx), writes=(kx,))
        region[0] = _sv

    def ffn_thunks(cp, rb):
        return ([lambda: emit_ln(cp, rb)] +
                [lambda mt=mt: ffn_mt(cp, rb, mt) for mt in range(NKT)])

    def emit_ffn(cp, rb):
        for t in ffn_thunks(cp, rb):
            t()

    xT_r = xT.rearrange("(k p) t -> p k t", p=128)
    out_r = out_fm.rearrange("(k p) t -> p k t", p=128)

    def emit_load(c):
        cp = c % 2
        stage("dx", lambda h, c=c, cp=cp: h.dma_start(
            out=x_sb[:, cp], in_=xT_r[:, :, c * TC:(c + 1) * TC]),
            writes=(f"x{cp}",))

    def pre_thunks(c, with_load=True):
        cp = c % 2
        kx, kn = f"x{cp}", f"xn{cp}"
        th = []
        if with_load:
            th.append(lambda: emit_load(c))
        if c > 0:
            th.append(lambda: stage(
                "dve", lambda h: h.tensor_copy(out=x_inner[:, :, 0:3],
                                               in_=x_inner[:, :, TC:TC + 3]),
                reads=("xi",), writes=("xi",)))
        th += ffn_thunks(cp, rowb)
        th.append(lambda: emit_ln(cp, rowb))

        def win_kt(hoff, kt, cp=cp, kn=kn):
            if kt == 0:
                stage("dwi", lambda h, hoff=hoff: h.dma_start(
                    out=winb[:, 0, :], in_=w_in16[hoff // 1024]),
                    reads=("w16",), writes=("winb0",))
            if kt + 1 < NKT:
                q = (kt + 1) % 2
                stage("dwi", lambda h, kt=kt, hoff=hoff, q=q: h.dma_start(
                    out=winb[:, q, :],
                    in_=w_in16[(kt + 1) * 4 + hoff // 1024]),
                    reads=("w16",), writes=(f"winb{q}",))

            def win_mm(h, kt=kt, cp=cp):
                wt = winb[:, kt % 2, :]
                ins = None
                for mi in range(8):
                    ins = h.matmul(
                        out=pb[mi // 2][:, (mi % 2) * TC:(mi % 2 + 1) * TC],
                        lhsT=wt[:, mi * 128:(mi + 1) * 128],
                        rhs=xn[:, cp, kt, :],
                        start=(kt == 0), stop=(kt == NKT - 1))
                return ins
            stage("pe", win_mm, reads=(kn, f"winb{kt % 2}"),
                  writes=("pb0", "pb1", "pb2", "pb3"))

        def win_evac(half, pas, cp=cp):
            mt0 = half * 16 + pas * 8
            if half == 0:
                def xi_evac(h, mt0=mt0):
                    ins = None
                    for mi in range(8):
                        ins = h.activation(
                            out=x_inner[:, mt0 + mi, 3:3 + TC],
                            in_=pb[mi // 2][:, (mi % 2) * TC:(mi % 2 + 1) * TC],
                            func=AF.Copy)
                    return ins
                stage("act", xi_evac, reads=("pb0", "pb1", "pb2", "pb3"),
                      writes=("xi",))
            else:
                def z_silu(h, mt0=mt0, cp=cp):
                    ins = None
                    for mi in range(8):
                        ins = h.activation(
                            out=sz[:, cp, mt0 - 16 + mi, :],
                            in_=pb[mi // 2][:, (mi % 2) * TC:(mi % 2 + 1) * TC],
                            func=AF.Silu)
                    return ins
                stage("act", z_silu, reads=("pb0", "pb1", "pb2", "pb3"),
                      writes=(f"sz{cp}",))

        for half in range(2):
            for pas in range(2):
                hoff = half * 2048 + pas * 1024
                for kt in range(NKT):
                    th.append(lambda hoff=hoff, kt=kt: win_kt(hoff, kt))
                th.append(lambda half=half, pas=pas: win_evac(half, pas))

        def conv_group(d0, cp=cp):
            def conv_mac(h, d0=d0, cp=cp):
                ins = None
                for dt in range(d0, d0 + 4):
                    xiv = x_inner[:, dt, :]
                    h.tensor_scalar(out=cacc[:, 0], in0=xiv[:, 0:TC],
                                    scalar1=rows[:, COL_CW + dt * 4:COL_CW + dt * 4 + 1],
                                    scalar2=rows[:, COL_CB + dt:COL_CB + dt + 1],
                                    op0=OP.mult, op1=OP.add)
                    h.scalar_tensor_tensor(
                        out=cacc[:, 1], in0=xiv[:, 1:1 + TC],
                        scalar=rows[:, COL_CW + dt * 4 + 1:COL_CW + dt * 4 + 2],
                        in1=cacc[:, 0], op0=OP.mult, op1=OP.add)
                    h.scalar_tensor_tensor(
                        out=cacc[:, 0], in0=xiv[:, 2:2 + TC],
                        scalar=rows[:, COL_CW + dt * 4 + 2:COL_CW + dt * 4 + 3],
                        in1=cacc[:, 1], op0=OP.mult, op1=OP.add)
                    ins = h.scalar_tensor_tensor(
                        out=x_conv[:, cp, dt, :], in0=xiv[:, 3:3 + TC],
                        scalar=rows[:, COL_CW + dt * 4 + 3:COL_CW + dt * 4 + 4],
                        in1=cacc[:, 0], op0=OP.mult, op1=OP.add)
                return ins
            stage("dve", conv_mac, reads=("xi", "rows", "cacc"),
                  writes=(f"xc{cp}", "cacc"))

        def conv_silu(d0, cp=cp):
            def c_silu(h, d0=d0, cp=cp):
                ins = None
                for dt in range(d0, d0 + 4):
                    ins = h.activation(out=x_conv[:, cp, dt, :],
                                       in_=x_conv[:, cp, dt, :], func=AF.Silu)
                return ins
            stage("act", c_silu, reads=(f"xc{cp}",), writes=(f"xc{cp}",))

        for d0 in range(0, NDT, 4):
            th.append(lambda d0=d0: conv_group(d0))
            th.append(lambda d0=d0: conv_silu(d0))

        def wx_all(cp=cp):
            def wx_mm(h, cp=cp):
                ins = None
                for kt in range(NDT):
                    ins = h.matmul(out=pb[4][0:96, TC:2 * TC], lhsT=wxT[:, kt, :],
                                   rhs=x_conv[:, cp, kt, :],
                                   start=(kt == 0), stop=(kt == NDT - 1))
                return ins
            stage("pe", wx_mm, reads=(f"xc{cp}", "wxT"), writes=("pb4",))
            stage("act", lambda h, cp=cp: h.activation(
                out=dtr[:, cp, :], in_=pb[4][0:64, TC:2 * TC], func=AF.Copy),
                reads=("pb4",), writes=(f"dtr{cp}",))
            stage("act", lambda h: h.activation(
                out=bc_sb[:], in_=pb[4][64:96, TC:2 * TC], func=AF.Copy),
                reads=("pb4",), writes=("bcs",))
            stage("db", lambda h: h.dma_start(out=bcflat, in_=bc_sb[:]),
                  reads=("bcs",), writes=("bcf",))

            def bc_bcast(h, cp=cp):
                h.dma_start(out=Brep[:, cp],
                            in_=bcflat[0:16, :].partition_broadcast(128))
                h.dma_start(out=Crep[:, cp],
                            in_=bcflat[16:32, :].partition_broadcast(128))
            stage("db", bc_bcast, reads=("bcf",),
                  writes=(f"Brep{cp}", f"Crep{cp}"), ndma=2)
        th.append(wx_all)
        return th

    dt0_pref = [False]

    def emit_scan(c, post, pre):
        cp = c % 2
        region[0] = "scan"
        q = list(post) + list(pre)
        done = [0]

        def drain(upto):
            upto = min(upto, len(q))
            while done[0] < upto:
                q[done[0]]()
                done[0] += 1

        def dt_path(dt, cpd=cp):
            p = dt % 2
            bk = f"pb{6 + p}"
            stage("pe", lambda h, dt=dt, p=p, cpd=cpd: h.matmul(
                out=pb[6 + p][:, 0:TC], lhsT=wdtT[:, dt * 128:(dt + 1) * 128],
                rhs=dtr[:, cpd, :], start=True, stop=True),
                reads=(f"dtr{cpd}", "wdtT"), writes=(bk,))

            def act1(h, dt=dt, p=p):
                h.activation(out=a_f[:], in_=pb[6 + p][:, 0:TC], func=AF.Abs,
                             bias=rows[:, COL_BDT + dt:COL_BDT + dt + 1])
                h.activation(out=e_h[:], in_=a_f[:], func=AF.Exp, scale=-1.0)
                return h.activation(out=l_h[:, p], in_=e_h[:], func=AF.Ln,
                                    bias=1.0)
            stage("act", act1, reads=(bk, "rows", "af", "eh"),
                  writes=("af", "eh", f"lh{p}"))

            def sp_fin(h, dt=dt, p=p, cpd=cpd):
                h.tensor_scalar(out=r_f[:], in0=pb[6 + p][:, 0:TC],
                                scalar1=rows[:, COL_BDT + dt:COL_BDT + dt + 1],
                                scalar2=0.0, op0=OP.add, op1=OP.max)
                h.tensor_tensor(out=dt_f[:, p], in0=r_f[:], in1=l_h[:, p],
                                op=OP.add)
                return h.tensor_tensor(out=dtu[:, p], in0=dt_f[:, p],
                                       in1=x_conv[:, cpd, dt, :], op=OP.mult)
            stage("dve", sp_fin, reads=(bk, f"lh{p}", f"xc{cpd}", "rows", "rf"),
                  writes=("rf", f"dtf{p}", f"dtu{p}"))

            def mk_dA(h, dt=dt, p=p):
                ins = None
                for n in range(DS):
                    col = dt * DS + n
                    ins = h.activation(out=dA[:, p, n, :], in_=dt_f[:, p],
                                       func=AF.Exp, scale=acol[:, col:col + 1])
                return ins
            stage("act", mk_dA, reads=(f"dtf{p}", "acol"), writes=(f"dA{p}",))

        def scan_block(dt):
            p = dt % 2

            def scan_pre(h, dt=dt, p=p):
                h.tensor_tensor(out=fixt[:, p].unsqueeze(2),
                                in0=dA[:, p, :, 0:1],
                                in1=s_prev[:, dt, :].unsqueeze(2), op=OP.mult)
                h.tensor_tensor(out=w_t[:, p],
                                in0=dtu[:, p].unsqueeze(1)
                                    .broadcast_to([128, DS, TC]),
                                in1=Brep[:, cp], op=OP.mult)
                h.tensor_tensor(out=w_t[:, p, :, 0:1], in0=w_t[:, p, :, 0:1],
                                in1=fixt[:, p].unsqueeze(2), op=OP.add)
                h.memset(dA[:, p, :, 0:1], 0.0)
                h.tensor_tensor_scan(out=s_t[:, p].rearrange("p n t -> p (n t)"),
                                     data0=dA[:, p].rearrange("p n t -> p (n t)"),
                                     data1=w_t[:, p].rearrange("p n t -> p (n t)"),
                                     initial=0.0, op0=OP.mult, op1=OP.add)
                return h.tensor_copy(out=s_prev[:, dt, :].unsqueeze(2),
                                     in_=s_t[:, p, :, TC - 1:TC])
            stage("dve", scan_pre,
                  reads=(f"dA{p}", f"dtu{p}", f"Brep{cp}", "sprev"),
                  writes=(f"wt{p}", f"st{p}", "sprev", f"dA{p}", f"fx{p}"))

        def scan_drain(dt):
            p = dt % 2

            def nred(h, dt=dt, p=p):
                fl = lambda ap: ap.rearrange("p n t -> p (n t)")
                h.tensor_tensor(out=fl(w_t[:, p]), in0=fl(s_t[:, p]),
                                in1=fl(Crep[:, cp]), op=OP.mult)
                h.tensor_tensor(out=fl(s_t[:, p, 0:8, :]),
                                in0=fl(w_t[:, p, 0:8, :]),
                                in1=fl(w_t[:, p, 8:16, :]), op=OP.add)
                h.tensor_tensor(out=fl(s_t[:, p, 8:12, :]),
                                in0=fl(s_t[:, p, 0:4, :]),
                                in1=fl(s_t[:, p, 4:8, :]), op=OP.add)
                h.tensor_tensor(out=fl(s_t[:, p, 12:14, :]),
                                in0=fl(s_t[:, p, 8:10, :]),
                                in1=fl(s_t[:, p, 10:12, :]), op=OP.add)
                h.tensor_tensor(out=y_ssm[:, p], in0=s_t[:, p, 12, :],
                                in1=s_t[:, p, 13, :], op=OP.add)
                h.tensor_tensor(out=ysz[:], in0=y_ssm[:, p],
                                in1=sz[:, cp, dt, :], op=OP.mult)
                return h.scalar_tensor_tensor(
                    out=ygate[:, dt, :], in0=x_conv[:, cp, dt, :],
                    scalar=rows[:, COL_D + dt:COL_D + dt + 1], in1=ysz[:],
                    op0=OP.mult, op1=OP.add)
            stage("dve", nred,
                  reads=(f"st{p}", f"Crep{cp}", f"sz{cp}", f"xc{cp}", "rows"),
                  writes=(f"wt{p}", f"st{p}", f"yssm{p}", f"yg{dt}"))

        if dt0_pref[0]:
            dt0_pref[0] = False
        else:
            dt_path(0)
        for k in range(NDT):
            if k + 1 < NDT:
                dt_path(k + 1)
            drain(-(-len(q) * (k + 1) // (NDT - 4)))
            scan_block(k)
            if k >= 1:
                scan_drain(k - 1)
            if k == NDT - 2 and pre:
                dt_path(0, cpd=1 - cp)
                dt0_pref[0] = True
        scan_drain(NDT - 1)
        drain(len(q))
        region[0] = "main"

    def post_thunks(c):
        cp = c % 2
        kx = f"x{cp}"
        th = []

        def wout_kt(kt):
            if kt == 0:
                stage("dw", lambda h: h.dma_start(out=wob[:, 0, :],
                                                  in_=wo16_r[:, 0, :]),
                      reads=("wo16",), writes=("wob0",))
            if kt + 1 < NDT:
                q = (kt + 1) % 2
                stage("dw", lambda h, kt=kt, q=q: h.dma_start(
                    out=wob[:, q, :], in_=wo16_r[:, kt + 1, :]),
                    reads=("wo16",), writes=(f"wob{q}",))

            def wout_mm(h, kt=kt):
                ins = None
                for mt in range(NKT):
                    ins = h.matmul(
                        out=pb[mt // 2][:, (mt % 2) * TC:(mt % 2 + 1) * TC],
                        lhsT=wob[:, kt % 2, mt * 128:(mt + 1) * 128],
                        rhs=ygate[:, kt, :],
                        start=(kt == 0), stop=(kt == NDT - 1))
                return ins
            stage("pe", wout_mm, reads=(f"yg{kt}", f"wob{kt % 2}"),
                  writes=tuple(f"pb{i}" for i in range(4)))
        for kt in range(NDT):
            th.append(lambda kt=kt: wout_kt(kt))

        def wout_add_t():
            def wout_add(h, cp=cp):
                ins = None
                for mt in range(NKT):
                    ins = h.tensor_tensor(
                        out=x_sb[:, cp, mt, :], in0=x_sb[:, cp, mt, :],
                        in1=pb[mt // 2][:, (mt % 2) * TC:(mt % 2 + 1) * TC],
                        op=OP.add)
                return ins
            stage("dve", wout_add,
                  reads=tuple(f"pb{i}" for i in range(4)) + (kx,), writes=(kx,))
        th.append(wout_add_t)
        th += ffn_thunks(cp, rowb2)
        th.append(lambda: stage("do", lambda h, c=c, cp=cp: h.dma_start(
            out=out_r[:, :, c * TC:(c + 1) * TC], in_=x_sb[:, cp]),
            reads=(kx,), writes=()))
        return th

    emit_load(0)
    for t in pre_thunks(0, with_load=False):
        t()
    post_q = []
    for c in range(NCH):
        pre = []
        if c + 1 < NCH:
            pre = pre_thunks(c + 1)
        emit_scan(c, post_q, pre)
        post_q = post_thunks(c)
    for t in post_q:
        t()

    final_waits = [(r, counts[r]) for r in DMA_ROLES if counts[r] > 0]
    stages.append(("dve", lambda h: h.memset(fixt[:, 0], 0.0),
                   final_waits, 1))
    counts["dve"] += 1

    DMA_SET = set(DMA_ROLES)

    def emit_for(ename, h):
        waited = {}
        for eng, fn, waits, ndma in stages:
            sel = "dma" if eng in DMA_SET else eng
            if sel != ename:
                continue
            for (wsem, wval) in waits:
                if waited.get(wsem, -1) >= wval:
                    continue
                h.wait_ge(sems[wsem], wval)
                waited[wsem] = wval
            if sel == "dma":
                proxy = DmaProxy(h, sems[eng])
                fn(proxy)
                assert proxy.n == ndma, f"dma stage issued {proxy.n} != {ndma}"
            else:
                ins = fn(h)
                assert ins is not None, "compute stage returned no instruction"
                ins.then_inc(sems[eng], 1)

    with es, nc.Block() as block:
        @block.sync
        def _(sync):
            emit_for("dma", sync)

        @block.tensor
        def _(tensor):
            emit_for("pe", tensor)

        @block.scalar
        def _(scalar):
            emit_for("act", scalar)

        @block.vector
        def _(vector):
            emit_for("dve", vector)

        @block.gpsimd
        def _(gpsimd):
            emit_for("pool", gpsimd)

    return nc


_NC_CACHE = None


def kernel(L0_attn_x, L0_attn_W_in, L0_attn_conv_w, L0_attn_conv_b,
           L0_attn_W_x, L0_attn_W_dt, L0_attn_b_dt, L0_attn_A_log, L0_attn_D,
           L0_attn_W_out, L0_ffn_weights, L0_ffn_scale, L0_ffn_bias):
    global _NC_CACHE
    x = np.ascontiguousarray(np.asarray(L0_attn_x, np.float32))

    def colpack(v, k):
        return np.ascontiguousarray(np.asarray(v, np.float32).reshape(k, 128).T)

    rows_c = np.zeros((128, 128), np.float32)
    rows_c[:, COL_D:COL_D + 16] = colpack(L0_attn_D, 16)
    rows_c[:, COL_CB:COL_CB + 16] = colpack(L0_attn_conv_b, 16)
    rows_c[:, COL_BDT:COL_BDT + 16] = colpack(L0_attn_b_dt, 16)
    cw = np.asarray(L0_attn_conv_w, np.float32)[:, 0, :]
    rows_c[:, COL_CW:COL_CW + 64] = cw.reshape(16, 128, 4).transpose(1, 0, 2) \
        .reshape(128, 64)
    rows_c[:, COL_SC:COL_SC + 8] = colpack(L0_ffn_scale, 8)
    rows_c[:, COL_BI:COL_BI + 8] = colpack(L0_ffn_bias, 8)

    alog_c = np.ascontiguousarray(
        np.asarray(L0_attn_A_log, np.float32).reshape(16, 128, 16)
        .transpose(1, 0, 2).reshape(128, 256))

    w_inT = np.ascontiguousarray(np.asarray(L0_attn_W_in, np.float32).T)
    w_outT = np.ascontiguousarray(np.asarray(L0_attn_W_out, np.float32).T)
    w_ffnT = np.ascontiguousarray(np.asarray(L0_ffn_weights, np.float32).T)
    w_xT = np.ascontiguousarray(np.asarray(L0_attn_W_x, np.float32).T)
    w_dtT = np.ascontiguousarray(np.asarray(L0_attn_W_dt, np.float32).T)

    if _NC_CACHE is None:
        _NC_CACHE = build_nc()
    nc = _NC_CACHE

    shared = {"w_inT": w_inT, "w_outT": w_outT, "w_ffnT": w_ffnT,
              "w_xT": w_xT, "w_dtT": w_dtT, "alog_c": alog_c, "rows_c": rows_c}
    in_maps = []
    for b in range(8):
        m = dict(shared)
        m["xT"] = np.ascontiguousarray(x[b].T)
        in_maps.append(m)

    res = run_bass_kernel_spmd(nc, in_maps, core_ids=list(range(8)))
    out = np.empty((8, T, DM), np.float32)
    for b in range(8):
        out[b] = np.asarray(res.results[b]["out_fm"]).T
    return out


# version 24 (final)
# speedup vs baseline: 1.1844x; 1.0122x over previous
import os
import numpy as np
import ml_dtypes
import concourse.bass as bass
import concourse.mybir as mybir
from concourse.bass_utils import run_bass_kernel_spmd

F32 = mybir.dt.float32
F16 = mybir.dt.float16
AF = mybir.ActivationFunctionType
OP = mybir.AluOpType

T = 2048
DM = 1024
DI = 2048
DS = 16
DR = 64
TC = 256
NCH = T // TC
NKT = DM // 128
NDT = DI // 128
EPS = 1e-5

COL_D, COL_CB, COL_BDT, COL_CW = 0, 16, 32, 48
COL_SC, COL_BI = 112, 120


class DmaProxy:
    def __init__(self, h, dsem):
        self.h = h
        self.dsem = dsem
        self.n = 0

    def dma_start(self, **kw):
        ins = self.h.dma_start(**kw)
        ins.then_inc(self.dsem, 16)
        self.n += 1
        return ins


def build_nc():
    nc = bass.Bass("TRN2", target_bir_lowering=False,
                   dynamic_dma_scratch_size=1024)

    xT = nc.dram_tensor("xT", [DM, T], F32, kind="ExternalInput").ap()
    w_inT = nc.dram_tensor("w_inT", [DM, 2 * DI], F32, kind="ExternalInput").ap()
    w_outT = nc.dram_tensor("w_outT", [DI, DM], F32, kind="ExternalInput").ap()
    w_ffnT = nc.dram_tensor("w_ffnT", [DM, DM], F32, kind="ExternalInput").ap()
    w_xT = nc.dram_tensor("w_xT", [DI, 96], F32, kind="ExternalInput").ap()
    w_dtT = nc.dram_tensor("w_dtT", [DR, DI], F32, kind="ExternalInput").ap()
    alog_c = nc.dram_tensor("alog_c", [128, NDT * DS], F32, kind="ExternalInput").ap()
    rows_c = nc.dram_tensor("rows_c", [128, 128], F32, kind="ExternalInput").ap()
    out_fm = nc.dram_tensor("out_fm", [DM, T], F32, kind="ExternalOutput").ap()

    MK_DEBUG = int(os.environ.get("MK_DEBUG", "0"))
    dbg = {}
    if MK_DEBUG:
        for nm, shape, dt in [("dbg_x1", [DM, TC], F32),
                              ("dbg_xc", [DI, TC], F16),
                              ("dbg_sz", [DI, TC], F16),
                              ("dbg_yg", [DI, TC], F16),
                              ("dbg_xn", [DM, TC], F16),
                              ("dbg_xi", [DI, TC + 3], F16)]:
            dbg[nm] = nc.dram_tensor(nm, shape, dt, kind="ExternalOutput").ap()
    bcflat = nc.dram_tensor("bcflat", [32, TC], F16, kind="Internal").ap()
    w_in16 = nc.dram_tensor("w_in16", [4 * NKT, 128, 1024], F16, kind="Internal").ap()
    w_out16 = nc.dram_tensor("w_out16", [DI, DM], F16, kind="Internal").ap()

    from contextlib import ExitStack
    es = ExitStack()
    S = lambda name, shape, dt: es.enter_context(nc.sbuf_tensor(name, shape, dt))

    winb = S("winb", [128, 2, 1024], F16)
    wob = S("wob", [128, 2, DM], F16)
    sgnT = S("sgnT", [128, NKT, DM], F16)
    wxT = S("wxT", [128, NDT, 96], F16)
    wdtT = S("wdtT", [DR, DI], F16)
    acol = S("acol", [128, NDT * DS], F32)
    rows = S("rows", [128, 128], F32)
    ones16 = S("ones16", [128, 1], F16)
    ones32r = S("ones32r", [1, 128], F32)

    x_sb = S("x_sb", [128, 2, NKT, TC], F32)
    stage_w = x_sb[:, 0].rearrange("p k t -> p (k t)")
    xbf = S("xbf", [128, NKT, TC], F16)
    xq = S("xq", [128, NKT, TC], F16)
    xn = S("xn", [128, 2, NKT, TC], F16)
    x_inner = S("x_inner", [128, NDT, TC + 3], F16)
    x_conv = S("x_conv", [128, 2, NDT, TC], F16)
    sz = S("sz", [128, 2, NDT, TC], F16)
    dtr = S("dtr", [DR, 2, TC], F16)
    bc_sb = S("bc_sb", [32, TC], F16)
    Brep = S("Brep", [128, 2, DS, TC], F16)
    Crep = S("Crep", [128, 2, DS, TC], F16)
    ygate = S("ygate", [128, NDT, TC], F16)
    dA = S("dA", [128, 2, DS, TC], F16)
    w_t = S("w_t", [128, 2, DS, TC], F16)
    s_t = S("s_t", [128, 2, DS, TC], F16)
    dt_f = S("dt_f", [128, 2, TC], F32)
    a_f = S("a_f", [128, TC], F32)
    e_h = S("e_h", [128, TC], F16)
    l_h = S("l_h", [128, 2, TC], F16)
    r_f = S("r_f", [128, TC], F32)
    dtu = S("dtu", [128, 2, TC], F16)
    y_ssm = S("y_ssm", [128, 2, TC], F16)
    ysz = S("ysz", [128, TC], F16)
    g1 = S("g1", [128, TC], F16)
    s_prev = S("s_prev", [128, NDT, DS], F16)
    rowb = S("rowb", [1, 768], F32)
    rowb2 = S("rowb2", [1, 768], F32)
    cacc = S("cacc", [128, 2, TC], F32)
    fixt = S("fixt", [128, 2, DS], F16)

    pb = [es.enter_context(nc.psum_tensor(f"pb{i}", [128, 512], F32))
          for i in range(8)]

    sems = {k: es.enter_context(nc.semaphore(f"sem_{k}"))
            for k in ["pe", "act", "dve", "pool",
                      "dx", "dw", "dwi", "db", "do"]}

    stages = []
    DMA_ROLES = ("dx", "dw", "dwi", "db", "do")
    counts = {"pe": 0, "act": 0, "dve": 0, "pool": 0,
              "dx": 0, "dw": 0, "dwi": 0, "db": 0, "do": 0}
    writer = {}
    readers = {}

    last_role = {}
    STRICT = int(os.environ.get("MK_STRICT", "0"))
    region = ["main"]
    prev_tok = [None]

    def stage(eng, fn, reads=(), writes=(), ndma=1):
        is_dma = eng in DMA_ROLES
        deps = set()
        if is_dma and eng in last_role:
            deps.add(last_role[eng])
        RELAX = os.environ.get("MK_RELAX", "")
        if RELAX:
            strict_here = region[0] != RELAX
        else:
            strict_here = (region[0] == "pre" or STRICT == 1 or
                           (STRICT == 2 and region[0] != "scan") or
                           (STRICT == 3 and region[0] == "scan"))
        if strict_here and prev_tok[0] is not None:
            deps.add(prev_tok[0])
        for b in reads:
            if b in writer:
                deps.add(writer[b])
        for b in writes:
            if b in writer:
                deps.add(writer[b])
            for t in readers.get(b, ()):
                deps.add(t)
        if is_dma:
            counts[eng] += 16 * ndma
        else:
            counts[eng] += 1
        tok = (eng, counts[eng])
        best = {}
        for (e, v) in deps:
            if e == eng and not is_dma:
                continue
            if e not in best or v > best[e]:
                best[e] = v
        waits = sorted(best.items())
        stages.append((eng, fn, waits, ndma))
        prev_tok[0] = tok
        if is_dma:
            last_role[eng] = tok
        for b in reads:
            readers.setdefault(b, []).append(tok)
        for b in writes:
            writer[b] = tok
            readers[b] = []
        return tok

    region[0] = 'pre'
    def d_loads(h):
        h.dma_start(out=acol[:], in_=alog_c)
        h.dma_start(out=rows[:], in_=rows_c)
    stage("dw", d_loads, reads=(), writes=("acol", "rows"), ndma=2)

    stage("act", lambda h: h.activation(out=acol[:], in_=acol[:], func=AF.Exp),
          reads=("acol",), writes=("acol",))

    def neg_and_halve(h):
        h.tensor_scalar(out=acol[:], in0=acol[:], scalar1=-1.0, scalar2=None,
                        op0=OP.mult)
        return h.tensor_scalar(out=rows[:, COL_SC:COL_BI + 8],
                               in0=rows[:, COL_SC:COL_BI + 8],
                               scalar1=0.5, scalar2=None, op0=OP.mult)
    stage("dve", neg_and_halve, reads=("acol", "rows"), writes=("acol", "rows"))

    def mk_consts(h):
        h.memset(ones16[:], 1.0)
        h.memset(ones32r[:], 1.0)
        h.memset(s_prev[:], 0.0)
        return h.memset(x_inner[:, :, 0:3], 0.0)
    stage("dve", mk_consts, writes=("ones", "sprev", "xi"))

    stage("dw", lambda h: h.dma_start(out=stage_w[0:DR, :], in_=w_dtT),
          writes=("stg",))
    stage("dve", lambda h: h.tensor_copy(out=wdtT[:], in_=stage_w[0:DR, :]),
          reads=("x0",), writes=("wdtT",))

    wx_r = w_xT.rearrange("(k p) e -> p k e", p=128)
    for j in range(0, NDT, 8):
        stage("dw", lambda h, j=j: h.dma_start(out=stage_w[:, 0:768],
                                               in_=wx_r[:, j:j + 8, :]),
              writes=("stg",))
        stage("dve", lambda h, j=j: h.tensor_copy(
            out=wxT[:, j:j + 8, :],
            in_=stage_w[:, 0:768].rearrange("p (k e) -> p k e", k=8)),
            reads=("x0",), writes=("wxT",))

    wo_r = w_outT.rearrange("(k p) e -> p k e", p=128)
    wo16_r = w_out16.rearrange("(k p) e -> p k e", p=128)
    for j in range(0, NDT, 2):
        stage("dw", lambda h, j=j: h.dma_start(out=stage_w,
                                               in_=wo_r[:, j:j + 2, :]),
              writes=("x0",))
        stage("dve", lambda h, j=j: h.tensor_copy(
            out=wob[:], in_=stage_w.rearrange("p (k e) -> p k e", k=2)),
            reads=("x0",), writes=("wob0", "wob1"))
        stage("dw", lambda h, j=j: h.dma_start(
            out=wo16_r[:, j:j + 2, :], in_=wob[:]),
            reads=("wob0", "wob1"), writes=("wo16",))

    wf_r = w_ffnT.rearrange("(k p) e -> p k e", p=128)
    for j in range(0, NKT, 2):
        stage("dw", lambda h, j=j: h.dma_start(out=stage_w,
                                               in_=wf_r[:, j:j + 2, :]),
              writes=("x0",))

        def wf_sgn(h, j=j):
            sv = stage_w.rearrange("p (k e) -> p k e", k=2)
            h.tensor_scalar(out=sgnT[:, j:j + 2, :], in0=sv, scalar1=0.0,
                            scalar2=None, op0=OP.is_ge)
            return h.tensor_scalar(out=sgnT[:, j:j + 2, :],
                                   in0=sgnT[:, j:j + 2, :], scalar1=2.0,
                                   scalar2=-1.0, op0=OP.mult, op1=OP.add)
        stage("dve", wf_sgn, reads=("x0",), writes=("sgnT",))

    wi_r = w_inT.rearrange("(k p) e -> p k e", p=128)

    for kt in range(NKT):
        for hh in range(4):
            stage("dw", lambda h, kt=kt, hh=hh: h.dma_start(
                out=stage_w[:, 0:1024], in_=wi_r[:, kt, hh * 1024:(hh + 1) * 1024]),
                writes=("x0",))
            stage("dve", lambda h: h.tensor_copy(out=winb[:, 0, :],
                                                 in_=stage_w[:, 0:1024]),
                  reads=("x0",), writes=("winb0",))
            stage("dw", lambda h, kt=kt, hh=hh: h.dma_start(
                out=w_in16[kt * 4 + hh],
                in_=winb[:, 0, :]),
                reads=("winb0",), writes=("w16",))

    region[0] = 'main'
    def emit_ln(cp, rb):
        _saved = region[0]; region[0] = 'ffnln'
        xv, bv, qv, nv = x_sb[:, cp], xbf[:], xq[:], xn[:, cp]
        kx, kb, kq, kn = f"x{cp}", "xbf", "xq", f"xn{cp}"
        rbk = "rowb" if rb is rowb else "rowb2"

        xq_keys = tuple(f"{kq}_{mt}" for mt in range(NKT))

        def cvt(h):
            h.tensor_copy(out=bv, in_=xv)
            return h.tensor_tensor(out=qv, in0=bv, in1=bv, op=OP.mult)
        stage("dve", cvt, reads=(kx,), writes=(kb,) + xq_keys)

        def lstats(h):
            for kt in range(NKT):
                h.matmul(out=pb[4][0:1, 0:TC], lhsT=ones16[:], rhs=bv[:, kt, :],
                         start=(kt == 0), stop=(kt == NKT - 1))
            ins = None
            for kt in range(NKT):
                ins = h.matmul(out=pb[4][0:1, TC:2 * TC], lhsT=ones16[:],
                               rhs=qv[:, kt, :],
                               start=(kt == 0), stop=(kt == NKT - 1))
            return ins
        stage("pe", lstats, reads=(kb, "ones") + xq_keys, writes=("pb4",))

        def rowmath(h, rb=rb):
            h.tensor_scalar(out=rb[0:1, 0:TC], in0=pb[4][0:1, 0:TC],
                            scalar1=1.0 / DM, scalar2=None, op0=OP.mult)
            h.tensor_scalar(out=rb[0:1, 512:512 + TC], in0=pb[4][0:1, TC:2 * TC],
                            scalar1=1.0 / DM, scalar2=None, op0=OP.mult)
            h.tensor_tensor(out=rb[0:1, 256:256 + TC], in0=rb[0:1, 0:TC],
                            in1=rb[0:1, 0:TC], op=OP.mult)
            h.tensor_tensor(out=rb[0:1, 256:256 + TC],
                            in0=rb[0:1, 512:512 + TC],
                            in1=rb[0:1, 256:256 + TC], op=OP.subtract)
            return h.tensor_scalar(out=rb[0:1, 256:256 + TC],
                                   in0=rb[0:1, 256:256 + TC],
                                   scalar1=EPS, scalar2=None, op0=OP.add)
        stage("dve", rowmath, reads=("pb4",), writes=(rbk,))

        stage("act", lambda h, rb=rb: h.activation(out=rb[0:1, 256:256 + TC],
                                                   in_=rb[0:1, 256:256 + TC],
                                                   func=AF.Ln),
              reads=(rbk,), writes=(rbk,))
        stage("act", lambda h, rb=rb: h.activation(out=rb[0:1, 256:256 + TC],
                                                   in_=rb[0:1, 256:256 + TC],
                                                   func=AF.Exp, scale=-0.5),
              reads=(rbk,), writes=(rbk,))
        stage("dve", lambda h, rb=rb: h.tensor_tensor(out=rb[0:1, 512:512 + TC],
                                                      in0=rb[0:1, 0:TC],
                                                      in1=rb[0:1, 256:256 + TC],
                                                      op=OP.mult),
              reads=(rbk,), writes=(rbk,))

        def reps(h, rb=rb):
            h.matmul(out=pb[5][:, 0:TC], lhsT=ones32r[:],
                     rhs=rb[0:1, 256:256 + TC], start=True, stop=True)
            return h.matmul(out=pb[5][:, TC:2 * TC], lhsT=ones32r[:],
                            rhs=rb[0:1, 512:512 + TC], start=True, stop=True)
        stage("pe", reps, reads=(rbk, "ones"), writes=("pb5",))

        def apply(h):
            ins = None
            for kt in range(NKT):
                h.tensor_tensor(out=nv[:, kt, :], in0=xv[:, kt, :],
                                in1=pb[5][:, 0:TC], op=OP.mult)
                ins = h.tensor_tensor(out=nv[:, kt, :], in0=nv[:, kt, :],
                                      in1=pb[5][:, TC:2 * TC], op=OP.subtract)
            return ins
        stage("dve", apply, reads=(kx, "pb5"), writes=(kn,))
        region[0] = _saved

    def ffn_mt(cp, rb, mt):
        _sv = region[0]; region[0] = 'ffnln'
        xv, qv, nv = x_sb[:, cp], xq[:], xn[:, cp]
        kx, kn = f"x{cp}", f"xn{cp}"
        bk = f"pb{4 + mt % 2}"

        def mm(h, mt=mt):
            ins = None
            for kt in range(NKT):
                ins = h.matmul(out=pb[4 + mt % 2][:, 0:TC],
                               lhsT=sgnT[:, kt, mt * 128:(mt + 1) * 128],
                               rhs=nv[:, kt, :],
                               start=(kt == 0), stop=(kt == NKT - 1))
            return ins
        stage("pe", mm, reads=(kn, "sgnT"), writes=(bk,))

        stage("act", lambda h, mt=mt: h.activation(
            out=qv[:, mt, :], in_=pb[4 + mt % 2][:, 0:TC], func=AF.Identity,
            scale=rows[:, COL_SC + mt:COL_SC + mt + 1],
            bias=rows[:, COL_BI + mt:COL_BI + mt + 1]),
            reads=(bk, "rows"), writes=(f"xq_{mt}",))

        def clipadd(h, mt=mt):
            h.tensor_scalar(out=qv[:, mt, :], in0=qv[:, mt, :],
                            scalar1=-50.0, scalar2=50.0,
                            op0=OP.max, op1=OP.min)
            return h.tensor_tensor(out=xv[:, mt, :], in0=xv[:, mt, :],
                                   in1=qv[:, mt, :], op=OP.add)
        stage("dve", clipadd, reads=(f"xq_{mt}", kx), writes=(kx,))
        region[0] = _sv

    def ffn_thunks(cp, rb):
        return ([lambda: emit_ln(cp, rb)] +
                [lambda mt=mt: ffn_mt(cp, rb, mt) for mt in range(NKT)])

    def emit_ffn(cp, rb):
        for t in ffn_thunks(cp, rb):
            t()

    xT_r = xT.rearrange("(k p) t -> p k t", p=128)
    out_r = out_fm.rearrange("(k p) t -> p k t", p=128)

    def emit_load(c):
        cp = c % 2
        stage("dx", lambda h, c=c, cp=cp: h.dma_start(
            out=x_sb[:, cp], in_=xT_r[:, :, c * TC:(c + 1) * TC]),
            writes=(f"x{cp}",))

    def pre_thunks(c, with_load=True):
        cp = c % 2
        kx, kn = f"x{cp}", f"xn{cp}"
        th = []
        if with_load:
            th.append(lambda: emit_load(c))
        if c > 0:
            th.append(lambda: stage(
                "dve", lambda h: h.tensor_copy(out=x_inner[:, :, 0:3],
                                               in_=x_inner[:, :, TC:TC + 3]),
                reads=("xi",), writes=("xi",)))
        th += ffn_thunks(cp, rowb)
        th.append(lambda: emit_ln(cp, rowb))

        def win_kt(hoff, kt, cp=cp, kn=kn):
            if kt == 0:
                stage("dwi", lambda h, hoff=hoff: h.dma_start(
                    out=winb[:, 0, :], in_=w_in16[hoff // 1024]),
                    reads=("w16",), writes=("winb0",))
            if kt + 1 < NKT:
                q = (kt + 1) % 2
                stage("dwi", lambda h, kt=kt, hoff=hoff, q=q: h.dma_start(
                    out=winb[:, q, :],
                    in_=w_in16[(kt + 1) * 4 + hoff // 1024]),
                    reads=("w16",), writes=(f"winb{q}",))

            def win_mm(h, kt=kt, cp=cp):
                wt = winb[:, kt % 2, :]
                ins = None
                for mi in range(8):
                    ins = h.matmul(
                        out=pb[mi // 2][:, (mi % 2) * TC:(mi % 2 + 1) * TC],
                        lhsT=wt[:, mi * 128:(mi + 1) * 128],
                        rhs=xn[:, cp, kt, :],
                        start=(kt == 0), stop=(kt == NKT - 1))
                return ins
            stage("pe", win_mm, reads=(kn, f"winb{kt % 2}"),
                  writes=("pb0", "pb1", "pb2", "pb3"))

        def win_evac(half, pas, cp=cp):
            mt0 = half * 16 + pas * 8
            if half == 0:
                def xi_evac(h, mt0=mt0):
                    ins = None
                    for mi in range(8):
                        ins = h.activation(
                            out=x_inner[:, mt0 + mi, 3:3 + TC],
                            in_=pb[mi // 2][:, (mi % 2) * TC:(mi % 2 + 1) * TC],
                            func=AF.Copy)
                    return ins
                stage("act", xi_evac, reads=("pb0", "pb1", "pb2", "pb3"),
                      writes=("xi",))
            else:
                def z_silu(h, mt0=mt0, cp=cp):
                    ins = None
                    for mi in range(8):
                        ins = h.activation(
                            out=sz[:, cp, mt0 - 16 + mi, :],
                            in_=pb[mi // 2][:, (mi % 2) * TC:(mi % 2 + 1) * TC],
                            func=AF.Silu)
                    return ins
                stage("act", z_silu, reads=("pb0", "pb1", "pb2", "pb3"),
                      writes=(f"sz{cp}",))

        for half in range(2):
            for pas in range(2):
                hoff = half * 2048 + pas * 1024
                for kt in range(NKT):
                    th.append(lambda hoff=hoff, kt=kt: win_kt(hoff, kt))
                th.append(lambda half=half, pas=pas: win_evac(half, pas))

        def conv_group(d0, cp=cp):
            def conv_mac(h, d0=d0, cp=cp):
                ins = None
                for dt in range(d0, d0 + 4):
                    xiv = x_inner[:, dt, :]
                    h.tensor_scalar(out=cacc[:, 0], in0=xiv[:, 0:TC],
                                    scalar1=rows[:, COL_CW + dt * 4:COL_CW + dt * 4 + 1],
                                    scalar2=rows[:, COL_CB + dt:COL_CB + dt + 1],
                                    op0=OP.mult, op1=OP.add)
                    h.scalar_tensor_tensor(
                        out=cacc[:, 1], in0=xiv[:, 1:1 + TC],
                        scalar=rows[:, COL_CW + dt * 4 + 1:COL_CW + dt * 4 + 2],
                        in1=cacc[:, 0], op0=OP.mult, op1=OP.add)
                    h.scalar_tensor_tensor(
                        out=cacc[:, 0], in0=xiv[:, 2:2 + TC],
                        scalar=rows[:, COL_CW + dt * 4 + 2:COL_CW + dt * 4 + 3],
                        in1=cacc[:, 1], op0=OP.mult, op1=OP.add)
                    ins = h.scalar_tensor_tensor(
                        out=x_conv[:, cp, dt, :], in0=xiv[:, 3:3 + TC],
                        scalar=rows[:, COL_CW + dt * 4 + 3:COL_CW + dt * 4 + 4],
                        in1=cacc[:, 0], op0=OP.mult, op1=OP.add)
                return ins
            stage("dve", conv_mac, reads=("xi", "rows", "cacc"),
                  writes=(f"xc{cp}", "cacc"))

        def conv_silu(d0, cp=cp):
            def c_silu(h, d0=d0, cp=cp):
                ins = None
                for dt in range(d0, d0 + 4):
                    ins = h.activation(out=x_conv[:, cp, dt, :],
                                       in_=x_conv[:, cp, dt, :], func=AF.Silu)
                return ins
            stage("act", c_silu, reads=(f"xc{cp}",), writes=(f"xc{cp}",))

        for d0 in range(0, NDT, 4):
            th.append(lambda d0=d0: conv_group(d0))
            th.append(lambda d0=d0: conv_silu(d0))

        def wx_all(cp=cp):
            def wx_mm(h, cp=cp):
                ins = None
                for kt in range(NDT):
                    ins = h.matmul(out=pb[4][0:96, TC:2 * TC], lhsT=wxT[:, kt, :],
                                   rhs=x_conv[:, cp, kt, :],
                                   start=(kt == 0), stop=(kt == NDT - 1))
                return ins
            stage("pe", wx_mm, reads=(f"xc{cp}", "wxT"), writes=("pb4",))
            stage("act", lambda h, cp=cp: h.activation(
                out=dtr[:, cp, :], in_=pb[4][0:64, TC:2 * TC], func=AF.Copy),
                reads=("pb4",), writes=(f"dtr{cp}",))
            stage("act", lambda h: h.activation(
                out=bc_sb[:], in_=pb[4][64:96, TC:2 * TC], func=AF.Copy),
                reads=("pb4",), writes=("bcs",))
            stage("db", lambda h: h.dma_start(out=bcflat, in_=bc_sb[:]),
                  reads=("bcs",), writes=("bcf",))

            def bc_bcast(h, cp=cp):
                h.dma_start(out=Brep[:, cp],
                            in_=bcflat[0:16, :].partition_broadcast(128))
                h.dma_start(out=Crep[:, cp],
                            in_=bcflat[16:32, :].partition_broadcast(128))
            stage("db", bc_bcast, reads=("bcf",),
                  writes=(f"Brep{cp}", f"Crep{cp}"), ndma=2)
        th.append(wx_all)
        return th

    dt0_pref = [False]

    def emit_scan(c, post, pre):
        cp = c % 2
        region[0] = "scan"
        q = list(post) + list(pre)
        done = [0]

        def drain(upto):
            upto = min(upto, len(q))
            while done[0] < upto:
                q[done[0]]()
                done[0] += 1

        def dt_path(dt, cpd=cp):
            p = dt % 2
            bk = f"pb{6 + p}"
            stage("pe", lambda h, dt=dt, p=p, cpd=cpd: h.matmul(
                out=pb[6 + p][:, 0:TC], lhsT=wdtT[:, dt * 128:(dt + 1) * 128],
                rhs=dtr[:, cpd, :], start=True, stop=True),
                reads=(f"dtr{cpd}", "wdtT"), writes=(bk,))

            def act1(h, dt=dt, p=p):
                h.activation(out=a_f[:], in_=pb[6 + p][:, 0:TC], func=AF.Abs,
                             bias=rows[:, COL_BDT + dt:COL_BDT + dt + 1])
                h.activation(out=e_h[:], in_=a_f[:], func=AF.Exp, scale=-1.0)
                return h.activation(out=l_h[:, p], in_=e_h[:], func=AF.Ln,
                                    bias=1.0)
            stage("act", act1, reads=(bk, "rows", "af", "eh"),
                  writes=("af", "eh", f"lh{p}"))

            def sp_fin(h, dt=dt, p=p, cpd=cpd):
                h.tensor_scalar(out=r_f[:], in0=pb[6 + p][:, 0:TC],
                                scalar1=rows[:, COL_BDT + dt:COL_BDT + dt + 1],
                                scalar2=0.0, op0=OP.add, op1=OP.max)
                h.tensor_tensor(out=dt_f[:, p], in0=r_f[:], in1=l_h[:, p],
                                op=OP.add)
                return h.tensor_tensor(out=dtu[:, p], in0=dt_f[:, p],
                                       in1=x_conv[:, cpd, dt, :], op=OP.mult)
            stage("dve", sp_fin, reads=(bk, f"lh{p}", f"xc{cpd}", "rows", "rf"),
                  writes=("rf", f"dtf{p}", f"dtu{p}"))

            def mk_dA(h, dt=dt, p=p):
                ins = None
                for n in range(DS):
                    col = dt * DS + n
                    ins = h.activation(out=dA[:, p, n, :], in_=dt_f[:, p],
                                       func=AF.Exp, scale=acol[:, col:col + 1])
                return ins
            stage("act", mk_dA, reads=(f"dtf{p}", "acol"), writes=(f"dA{p}",))

        def scan_block(dt):
            p = dt % 2

            def scan_pre(h, dt=dt, p=p):
                h.tensor_tensor(out=fixt[:, p].unsqueeze(2),
                                in0=dA[:, p, :, 0:1],
                                in1=s_prev[:, dt, :].unsqueeze(2), op=OP.mult)
                h.tensor_tensor(out=w_t[:, p],
                                in0=dtu[:, p].unsqueeze(1)
                                    .broadcast_to([128, DS, TC]),
                                in1=Brep[:, cp], op=OP.mult)
                h.tensor_tensor(out=w_t[:, p, :, 0:1], in0=w_t[:, p, :, 0:1],
                                in1=fixt[:, p].unsqueeze(2), op=OP.add)
                h.memset(dA[:, p, :, 0:1], 0.0)
                h.tensor_tensor_scan(out=s_t[:, p].rearrange("p n t -> p (n t)"),
                                     data0=dA[:, p].rearrange("p n t -> p (n t)"),
                                     data1=w_t[:, p].rearrange("p n t -> p (n t)"),
                                     initial=0.0, op0=OP.mult, op1=OP.add)
                return h.tensor_copy(out=s_prev[:, dt, :].unsqueeze(2),
                                     in_=s_t[:, p, :, TC - 1:TC])
            stage("dve", scan_pre,
                  reads=(f"dA{p}", f"dtu{p}", f"Brep{cp}", "sprev"),
                  writes=(f"wt{p}", f"st{p}", "sprev", f"dA{p}", f"fx{p}"))

        def scan_drain(dt):
            p = dt % 2

            def nred(h, dt=dt, p=p):
                fl = lambda ap: ap.rearrange("p n t -> p (n t)")
                h.tensor_tensor(out=fl(w_t[:, p]), in0=fl(s_t[:, p]),
                                in1=fl(Crep[:, cp]), op=OP.mult)
                h.tensor_tensor(out=fl(s_t[:, p, 0:8, :]),
                                in0=fl(w_t[:, p, 0:8, :]),
                                in1=fl(w_t[:, p, 8:16, :]), op=OP.add)
                h.tensor_tensor(out=fl(s_t[:, p, 8:12, :]),
                                in0=fl(s_t[:, p, 0:4, :]),
                                in1=fl(s_t[:, p, 4:8, :]), op=OP.add)
                h.tensor_tensor(out=fl(s_t[:, p, 12:14, :]),
                                in0=fl(s_t[:, p, 8:10, :]),
                                in1=fl(s_t[:, p, 10:12, :]), op=OP.add)
                h.tensor_tensor(out=y_ssm[:, p], in0=s_t[:, p, 12, :],
                                in1=s_t[:, p, 13, :], op=OP.add)
                h.tensor_tensor(out=ysz[:], in0=y_ssm[:, p],
                                in1=sz[:, cp, dt, :], op=OP.mult)
                return h.scalar_tensor_tensor(
                    out=ygate[:, dt, :], in0=x_conv[:, cp, dt, :],
                    scalar=rows[:, COL_D + dt:COL_D + dt + 1], in1=ysz[:],
                    op0=OP.mult, op1=OP.add)
            stage("dve", nred,
                  reads=(f"st{p}", f"Crep{cp}", f"sz{cp}", f"xc{cp}", "rows"),
                  writes=(f"wt{p}", f"st{p}", f"yssm{p}", f"yg{dt}"))

        if dt0_pref[0]:
            dt0_pref[0] = False
        else:
            dt_path(0)
        for k in range(NDT):
            if k + 1 < NDT:
                dt_path(k + 1)
            drain(-(-len(q) * (k + 1) // (NDT - 6)))
            scan_block(k)
            if k >= 1:
                scan_drain(k - 1)
            if k == NDT - 2 and pre:
                dt_path(0, cpd=1 - cp)
                dt0_pref[0] = True
        scan_drain(NDT - 1)
        drain(len(q))
        region[0] = "main"

    def post_thunks(c):
        cp = c % 2
        kx = f"x{cp}"
        th = []

        def wout_kt(kt):
            if kt == 0:
                stage("dw", lambda h: h.dma_start(out=wob[:, 0, :],
                                                  in_=wo16_r[:, 0, :]),
                      reads=("wo16",), writes=("wob0",))
            if kt + 1 < NDT:
                q = (kt + 1) % 2
                stage("dw", lambda h, kt=kt, q=q: h.dma_start(
                    out=wob[:, q, :], in_=wo16_r[:, kt + 1, :]),
                    reads=("wo16",), writes=(f"wob{q}",))

            def wout_mm(h, kt=kt):
                ins = None
                for mt in range(NKT):
                    ins = h.matmul(
                        out=pb[mt // 2][:, (mt % 2) * TC:(mt % 2 + 1) * TC],
                        lhsT=wob[:, kt % 2, mt * 128:(mt + 1) * 128],
                        rhs=ygate[:, kt, :],
                        start=(kt == 0), stop=(kt == NDT - 1))
                return ins
            stage("pe", wout_mm, reads=(f"yg{kt}", f"wob{kt % 2}"),
                  writes=tuple(f"pb{i}" for i in range(4)))
        for kt in range(NDT):
            th.append(lambda kt=kt: wout_kt(kt))

        def wout_add_t():
            def wout_add(h, cp=cp):
                ins = None
                for mt in range(NKT):
                    ins = h.tensor_tensor(
                        out=x_sb[:, cp, mt, :], in0=x_sb[:, cp, mt, :],
                        in1=pb[mt // 2][:, (mt % 2) * TC:(mt % 2 + 1) * TC],
                        op=OP.add)
                return ins
            stage("dve", wout_add,
                  reads=tuple(f"pb{i}" for i in range(4)) + (kx,), writes=(kx,))
        th.append(wout_add_t)
        th += ffn_thunks(cp, rowb2)
        th.append(lambda: stage("do", lambda h, c=c, cp=cp: h.dma_start(
            out=out_r[:, :, c * TC:(c + 1) * TC], in_=x_sb[:, cp]),
            reads=(kx,), writes=()))
        return th

    emit_load(0)
    for t in pre_thunks(0, with_load=False):
        t()
    post_q = []
    for c in range(NCH):
        pre = []
        if c + 1 < NCH:
            pre = pre_thunks(c + 1)
        emit_scan(c, post_q, pre)
        post_q = post_thunks(c)
    for t in post_q:
        t()

    final_waits = [(r, counts[r]) for r in DMA_ROLES if counts[r] > 0]
    stages.append(("dve", lambda h: h.memset(fixt[:, 0], 0.0),
                   final_waits, 1))
    counts["dve"] += 1

    DMA_SET = set(DMA_ROLES)

    def emit_for(ename, h):
        waited = {}
        for eng, fn, waits, ndma in stages:
            sel = "dma" if eng in DMA_SET else eng
            if sel != ename:
                continue
            for (wsem, wval) in waits:
                if waited.get(wsem, -1) >= wval:
                    continue
                h.wait_ge(sems[wsem], wval)
                waited[wsem] = wval
            if sel == "dma":
                proxy = DmaProxy(h, sems[eng])
                fn(proxy)
                assert proxy.n == ndma, f"dma stage issued {proxy.n} != {ndma}"
            else:
                ins = fn(h)
                assert ins is not None, "compute stage returned no instruction"
                ins.then_inc(sems[eng], 1)

    with es, nc.Block() as block:
        @block.sync
        def _(sync):
            emit_for("dma", sync)

        @block.tensor
        def _(tensor):
            emit_for("pe", tensor)

        @block.scalar
        def _(scalar):
            emit_for("act", scalar)

        @block.vector
        def _(vector):
            emit_for("dve", vector)

        @block.gpsimd
        def _(gpsimd):
            emit_for("pool", gpsimd)

    return nc


_NC_CACHE = None


def kernel(L0_attn_x, L0_attn_W_in, L0_attn_conv_w, L0_attn_conv_b,
           L0_attn_W_x, L0_attn_W_dt, L0_attn_b_dt, L0_attn_A_log, L0_attn_D,
           L0_attn_W_out, L0_ffn_weights, L0_ffn_scale, L0_ffn_bias):
    global _NC_CACHE
    x = np.ascontiguousarray(np.asarray(L0_attn_x, np.float32))

    def colpack(v, k):
        return np.ascontiguousarray(np.asarray(v, np.float32).reshape(k, 128).T)

    rows_c = np.zeros((128, 128), np.float32)
    rows_c[:, COL_D:COL_D + 16] = colpack(L0_attn_D, 16)
    rows_c[:, COL_CB:COL_CB + 16] = colpack(L0_attn_conv_b, 16)
    rows_c[:, COL_BDT:COL_BDT + 16] = colpack(L0_attn_b_dt, 16)
    cw = np.asarray(L0_attn_conv_w, np.float32)[:, 0, :]
    rows_c[:, COL_CW:COL_CW + 64] = cw.reshape(16, 128, 4).transpose(1, 0, 2) \
        .reshape(128, 64)
    rows_c[:, COL_SC:COL_SC + 8] = colpack(L0_ffn_scale, 8)
    rows_c[:, COL_BI:COL_BI + 8] = colpack(L0_ffn_bias, 8)

    alog_c = np.ascontiguousarray(
        np.asarray(L0_attn_A_log, np.float32).reshape(16, 128, 16)
        .transpose(1, 0, 2).reshape(128, 256))

    w_inT = np.ascontiguousarray(np.asarray(L0_attn_W_in, np.float32).T)
    w_outT = np.ascontiguousarray(np.asarray(L0_attn_W_out, np.float32).T)
    w_ffnT = np.ascontiguousarray(np.asarray(L0_ffn_weights, np.float32).T)
    w_xT = np.ascontiguousarray(np.asarray(L0_attn_W_x, np.float32).T)
    w_dtT = np.ascontiguousarray(np.asarray(L0_attn_W_dt, np.float32).T)

    if _NC_CACHE is None:
        _NC_CACHE = build_nc()
    nc = _NC_CACHE

    shared = {"w_inT": w_inT, "w_outT": w_outT, "w_ffnT": w_ffnT,
              "w_xT": w_xT, "w_dtT": w_dtT, "alog_c": alog_c, "rows_c": rows_c}
    in_maps = []
    for b in range(8):
        m = dict(shared)
        m["xT"] = np.ascontiguousarray(x[b].T)
        in_maps.append(m)

    res = run_bass_kernel_spmd(nc, in_maps, core_ids=list(range(8)))
    out = np.empty((8, T, DM), np.float32)
    for b in range(8):
        out[b] = np.asarray(res.results[b]["out_fm"]).T
    return out
